# version 24
# baseline (speedup 1.0000x reference)
import contextlib
import numpy as np
import concourse.bass as bass
import concourse.mybir as mybir
from concourse.ap import AP
from concourse.bass_utils import run_bass_kernel_spmd

F32 = mybir.dt.float32
BF16 = mybir.dt.bfloat16
AF = mybir.ActivationFunctionType
ALU = mybir.AluOpType
X = mybir.AxisListType.X

MAGIC = 12582912.0
D = 1024
K = 32
INNER = 4096
T = 2048
TP = 2176
NT = TP // 128
NBLK = 16
NSB = 4
BPS = 4
IC = 8
OC = 32
OB = 8
MW = 160


class Builder:

    def __init__(self):
        self.events = []
        self.cnt = {}

    def sem_names(self):
        return sorted(self.cnt)

    def ev(self, eng, fn, waits=(), incs=()):
        w = {}
        for name, val in waits:
            w[name] = max(w.get(name, 0), val)
        for name, n in incs:
            self.cnt[name] = self.cnt.get(name, 0) + n
        self.events.append((eng, fn, sorted(w.items()), list(incs)))

    def dma(self, eng, sem, out, in_, waits=(), transpose=False):
        self.cnt.setdefault(sem, 0)
        if transpose:
            fn = lambda e, o=out, i=in_: e.dma_start_transpose(o, i)
        else:
            fn = lambda e, o=out, i=in_: e.dma_start(out=o, in_=i)
        self.ev(eng, fn, waits, incs=[(sem, 16)])
        return self.cnt[sem]

    def comp(self, eng, sem, fn, waits=(), inc=1, sw=True):
        self.cnt.setdefault(sem, 0)
        waits = list(waits)
        if sw and eng in ("vector", "scalar", "gpsimd"):
            waits.append((sem, self.cnt[sem]))
        self.ev(eng, fn, waits, incs=([(sem, inc)] if inc else []))
        return self.cnt[sem]

    def at(self, sem):
        return (sem, self.cnt.get(sem, 0))


def build_core_program(debug=False):
    nc = bass.Bass("TRN2", target_bir_lowering=False)
    ikind = "ExternalOutput" if debug else "Internal"

    xs = nc.dram_tensor("xs", [TP, D], F32, kind="ExternalInput").ap()
    Wm = nc.dram_tensor("Wm", [K, D], F32, kind="ExternalInput").ap()
    bb = nc.dram_tensor("bb", [128, K], F32, kind="ExternalInput").ap()
    wg = nc.dram_tensor("wg", [INNER, D], F32, kind="ExternalInput").ap()
    wu = nc.dram_tensor("wu", [INNER, D], F32, kind="ExternalInput").ap()
    wd = nc.dram_tensor("wd", [D, INNER], F32, kind="ExternalInput").ap()
    yo = nc.dram_tensor("yo", [T, D], F32, kind="ExternalOutput").ap()

    xbf_img = nc.dram_tensor("xbf_img", [TP, D], BF16, kind=ikind)
    hq_img = nc.dram_tensor("hq_img", [T, D], BF16, kind=ikind)
    wg_img = nc.dram_tensor("wg_img", [INNER, D], BF16, kind=ikind)
    wu_img = nc.dram_tensor("wu_img", [INNER, D], BF16, kind=ikind)
    wd_img = nc.dram_tensor("wd_img", [D, INNER], BF16, kind=ikind)
    wm_img = nc.dram_tensor("wm_img", [128, D], BF16, kind="Internal")
    mimg = [nc.dram_tensor(f"mimg{j}", [(MW + 1) * 128], BF16, kind="Internal")
            for j in range(4)]
    h_dbg = nc.dram_tensor("h_dbg", [T, D], F32, kind=ikind) if debug else None
    sm_dbg = nc.dram_tensor("sm_dbg", [T, K], F32, kind=ikind) if debug else None
    P_dbg = nc.dram_tensor("P_dbg", [NSB, 128, OC * 512], F32, kind=ikind) if debug else None
    st_dbg = nc.dram_tensor("st_dbg", [1, 32], F32, kind=ikind) if debug else None
    sw_dbg = nc.dram_tensor("sw_dbg", [6, 128], F32, kind=ikind) if debug else None
    rc_dbg = nc.dram_tensor("rc_dbg", [3, 128, 64], F32, kind=ikind) if debug else None
    iq_dbg = nc.dram_tensor("iq_dbg", [NSB, 128, OC * 512], F32, kind=ikind) if debug else None

    B = Builder()
    es = contextlib.ExitStack()
    with es:
        def sb_(name, shape, dt):
            return es.enter_context(nc.sbuf_tensor(name, shape, dt))

        ones_f = sb_("ones_f", [128, 128], F32)
        ident_f = sb_("ident_f", [128, 128], F32)
        onescol = sb_("onescol", [128, 1], F32)
        b_bc = sb_("b_bc", [128, K], F32)
        eps_col = sb_("eps_col", [128, 1], F32)
        nhalf_col = sb_("nhalf_col", [128, 1], F32)
        phalf_col = sb_("phalf_col", [128, 1], F32)
        magic_col = sb_("magic_col", [128, 1], F32)
        nmagic_col = sb_("nmagic_col", [128, 1], F32)
        zeros_m = sb_("zeros_m", [128, MW + 1], BF16)
        Wt_full = sb_("Wt_full", [128, IC * 128], BF16)
        swrep = [sb_(f"swrep{i}", [128, 1], F32) for i in range(3)]
        sirep = [sb_(f"sirep{i}", [128, 1], F32) for i in range(3)]
        wstat = sb_("wstat", [1, 32], F32)

        wch = [sb_(f"wch{j}", [128, 512], F32) for j in range(4)]
        fq = [sb_(f"fq{j}", [128, 512], F32) for j in range(2)]
        tq_bf = [sb_(f"tq_bf{j}", [128, 512], BF16) for j in range(2)]
        rcols = sb_("rcols", [128, 64], F32)
        abs_dump = sb_("abs_dump", [128, 512], F32)
        r8 = sb_("r8", [128, 8], F32)
        pc_col = sb_("pc_col", [128, 1], F32)
        ones8 = sb_("ones8", [128, 8], F32)
        W_f = [wch[0][0:K, :], wch[1][0:K, :]]
        W_bt = sb_("W_bt", [K, D], BF16)

        wqTg = sb_("wqTg", [128, IC * INNER], BF16)
        wuT = [sb_(f"wuT{j}", [128, IC * 512], BF16) for j in range(2)]
        wdT = [sb_(f"wdT{j}", [128, D], BF16) for j in range(2)]
        P_sb = sb_("P_sb", [128, OC * 512], BF16)
        h_sb = sb_("h_sb", [128, BPS * D], F32)
        fmbuf = sb_("fmbuf", [128, IC * 512], BF16)

        xf32 = [sb_(f"xf32_{j}", [128, D], F32) for j in range(2)]
        xbf = [sb_(f"xbf_{j}", [128, D], BF16) for j in range(2)]
        ytile = [sb_(f"ytile{j}", [128, D], F32) for j in range(2)]
        r_t = sb_("r_t", [128, D], F32)
        u_scr = sb_("u_scr", [128, D], F32)
        hq_t = sb_("hq_t", [128, D], BF16)
        gated = sb_("gated", [128, 512], F32)
        silu_g = sb_("silu_g", [128, 512], F32)
        amax_acc = sb_("amax_acc", [128, 512], F32)
        maxac = sb_("maxac", [128, 512], BF16)
        minac = sb_("minac", [128, 512], BF16)
        u1_s = silu_g
        cgb = sb_("cgb", [128, 512], F32)
        qmb = sb_("qmb", [128, 512], F32)
        bb_scr = sb_("bb_scr", [128, 128], F32)
        m_tm = sb_("m_tm", [128, MW], BF16)
        m_tf = sb_("m_tf", [128, MW], F32)
        m1t = sb_("m1t", [128, 128], BF16)
        m2t = sb_("m2t", [32, 128], BF16)
        w_sm = sb_("w_sm", [128, BPS * K], BF16)
        logit_s = sb_("logit_s", [128, K], F32)
        expv = sb_("expv", [128, K], F32)
        col = {name: sb_(f"c_{name}", [128, 1], F32) for name in
               ["sum", "mean", "sq", "msq", "m2c", "var", "stdv", "rstd",
                "amx", "t0", "t1", "s_col", "negmax", "sumexp", "rsx"]}
        sinv_sb = sb_("sinv_sb", [128, BPS], F32)
        cg_sb = sb_("cg_sb", [128, BPS], F32)
        cu_sb = sb_("cu_sb", [128, BPS], F32)
        cd_sb = sb_("cd_sb", [128, BPS], F32)
        amp_sb = sb_("amp_sb", [128, BPS], F32)
        s2_sb = sb_("s2_sb", [128, BPS], F32)
        s2i_sb = sb_("s2i_sb", [128, BPS], F32)
        qm_sb = sb_("qm_sb", [128, BPS], F32)

        PB = [es.enter_context(nc.psum_tensor(f"PB{j}", [128, 512], F32))
              for j in range(8)]

        sems = {}
        semnames = (["qc", "zw", "swt", "swg", "sc0", "sfm", "shq", "dbg",
                     "ts", "vs", "ss", "gs"]
                    + [f"w{j}" for j in range(4)]
                    + ["st0", "st1", "sy0", "sy1", "xa0", "xa1", "xb0", "xb1",
                       "su0", "su1", "sd0", "sd1"]
                    + [f"sm{j}" for j in range(4)] + [f"sr{j}" for j in range(4)])
        for s in semnames:
            sems[s] = es.enter_context(nc.semaphore(s))
            B.cnt[s] = 0

        B.comp("vector", "vs", lambda e: e.memset(ones_f[:], 1.0))
        B.comp("vector", "vs", lambda e: e.memset(onescol[:], 1.0))
        B.comp("vector", "vs", lambda e: e.memset(eps_col[:], 1e-5))
        B.comp("vector", "vs", lambda e: e.memset(nhalf_col[:], -0.5))
        B.comp("vector", "vs", lambda e: e.memset(phalf_col[:], 0.5))
        B.comp("vector", "vs", lambda e: e.memset(magic_col[:], MAGIC))
        B.comp("vector", "vs", lambda e: e.memset(nmagic_col[:], -MAGIC))
        B.comp("vector", "vs", lambda e: e.memset(zeros_m[:], 0.0))
        B.comp("vector", "vs", lambda e: e.memset(tq_bf[0][:], 0.0))
        B.comp("vector", "vs", lambda e: e.memset(tq_bf[1][:], 0.0))
        v_consts = B.at("vs")
        B.comp("gpsimd", "gs", lambda e: e.affine_select(
            ident_f[:], ones_f[:], pattern=[[-1, 128]], compare_op=ALU.is_equal,
            fill=0.0, base=0, channel_multiplier=1), waits=[v_consts])
        B.comp("gpsimd", "gs", lambda e: e.affine_select(
            ones8[:], ones_f[:, 0:8], pattern=[[-16, 8]], compare_op=ALU.is_ge,
            fill=0.0, base=0, channel_multiplier=1), waits=[v_consts])
        B.comp("gpsimd", "gs", lambda e: e.affine_select(
            ones8[:], ones8[:], pattern=[[16, 8]], compare_op=ALU.is_ge,
            fill=0.0, base=15, channel_multiplier=-1))
        g_ident = B.at("gs")
        B.dma("gpsimd", "qc", xbf_img.ap(), xs[:, :])
        mread = {}
        for j in range(4):
            B.dma("gpsimd", f"sr{j}", AP(mimg[j], 0, [[1, (MW + 1) * 128]]),
                  zeros_m[:], waits=[v_consts])
            mread[j - 4] = B.at(f"sr{j}")
        B.dma("sync", "sc0", W_f[0], Wm[:, 0:512])
        B.dma("sync", "sc0", W_f[1], Wm[:, 512:1024])
        B.comp("vector", "vs", lambda e: e.tensor_copy(out=W_bt[:, 0:512],
               in_=W_f[0]), waits=[B.at("sc0")])
        B.comp("vector", "vs", lambda e: e.tensor_copy(out=W_bt[:, 512:1024],
               in_=W_f[1]))
        B.dma("sync", "zw", wm_img.ap()[0:128, 0:512], tq_bf[0][:], waits=[v_consts])
        B.dma("sync", "zw", wm_img.ap()[0:128, 512:1024], tq_bf[1][:], waits=[v_consts])
        B.dma("sync", "zw", wm_img.ap()[0:K, :], W_bt[:], waits=[B.at("vs"), B.at("zw")])
        w_img_done = B.at("zw")
        for c in range(IC):
            B.dma("sync", "swt", Wt_full[:, 128 * c:128 * (c + 1)],
                  wm_img.ap()[0:128, 128 * c:128 * (c + 1)],
                  waits=[w_img_done], transpose=True)
        wt_done = B.at("swt")
        B.dma("sync", "sc0", b_bc[:], bb[:, :])
        p0_qx = B.at("sc0")

        wsrc = [wg, wu, wd]
        wimg = [wg_img, wu_img, wd_img]

        def wchunks(widx):
            if widx < 2:
                return [(128 * c, 128, 512 * s, 512) for c in range(32)
                        for s in range(2)]
            return [(128 * r, 128, 512 * s, 512) for r in range(8) for s in range(8)]

        p1s = {"done": {j: B.at("vs") for j in range(4)}, "prev": B.at("vs")}

        def p1_chunk(widx, ci, r0, nr, c0, ncol):
            slot = ci % 4
            B.dma("sync", f"w{slot}" if slot < 4 else "w0", wch[slot][:],
                  wsrc[widx][r0:r0 + nr, c0:c0 + ncol],
                  waits=[p1s["done"][slot]])
            ld = B.at(f"w{slot}")
            if slot < 2:
                B.comp("vector", "vs", lambda e, ci=ci, slot=slot: e.reduce_sum(
                    out=rcols[:, ci:ci + 1], in_=wch[slot][:], axis=X,
                    apply_absolute_value=True), waits=[ld, p1s["prev"]],
                    sw=False)
                p1s["done"][slot] = B.at("vs")
            else:
                B.comp("scalar", "ss", lambda e, ci=ci, slot=slot: e.activation(
                    out=abs_dump[:], in_=wch[slot][:], func=AF.Abs, bias=0.0,
                    scale=1.0, accum_out=rcols[:, ci:ci + 1]),
                    waits=[ld, p1s["prev"]])
                p1s["done"][slot] = B.at("ss")

        def p1_stats(widx):
            B.comp("vector", "vs", lambda e: e.reduce_sum(
                out=r8[:].rearrange("p (a b) -> p a b", b=1),
                in_=rcols[:].rearrange("p (a b) -> p a b", b=8), axis=X),
                waits=[B.at("ss")])
            B.comp("vector", "vs", lambda e: e.reduce_sum(
                out=pc_col[:], in_=r8[:], axis=X))
            v_acc = B.at("vs")
            if debug:
                B.dma("sync", "dbg", rc_dbg.ap()[widx, :, :], rcols[:],
                      waits=[v_acc])
            B.comp("tensor", "ts", lambda e: e.matmul(
                PB[7][0:1, 0:8], pc_col[:], ones8[:], start=True, stop=True),
                waits=[v_acc, g_ident, B.at("vs")])
            B.comp("vector", "vs", lambda e: e.tensor_copy(
                out=wstat[0:1, 16:24], in_=PB[7][0:1, 0:8]), waits=[B.at("ts")])
            B.comp("vector", "vs", lambda e: e.reduce_sum(
                out=wstat[0:1, 15:16], in_=wstat[0:1, 16:24], axis=X))
            w8 = widx * 5
            B.comp("vector", "vs", lambda e, o=wstat[0:1, w8:w8 + 1]:
                   e.tensor_scalar(
                out=o, in0=wstat[0:1, 15:16], scalar1=float(2.0 ** -22),
                scalar2=None, op0=ALU.mult))
            B.comp("vector", "vs", lambda e, i=wstat[0:1, w8:w8 + 1],
                   o=wstat[0:1, w8 + 1:w8 + 2]: e.tensor_scalar(
                out=o, in0=i, scalar1=1e-5, scalar2=None, op0=ALU.max))
            B.comp("vector", "vs", lambda e, i=wstat[0:1, w8 + 1:w8 + 2],
                   o=wstat[0:1, w8 + 2:w8 + 3]: e.reciprocal(out=o, in_=i))
            B.comp("vector", "vs", lambda e, i=wstat[0:1, w8 + 2:w8 + 3],
                   o=wstat[0:1, w8 + 3:w8 + 4]: e.reciprocal(out=o, in_=i))
            v_stats = B.at("vs")
            for which, dst in ((2, swrep[widx]), (3, sirep[widx])):
                B.comp("tensor", "ts",
                       lambda e, wv=wstat[0:1, w8 + which:w8 + which + 1]:
                       e.matmul(PB[7][:, 0:1], ones_f[0:1, 0:128], wv,
                                start=True, stop=True),
                       waits=[v_stats, B.at("vs")])
                B.comp("vector", "vs", lambda e, d=dst: e.tensor_copy(
                    out=d[:], in_=PB[7][:, 0:1]), waits=[B.at("ts")])
            p1s["prev"] = B.at("vs")

        ab_free = {0: B.at("zw"), 1: B.at("zw")}
        wr_done = {0: B.at("zw"), 1: B.at("zw")}
        ld_free = None

        def p2_chunk(widx, ci, r0, nr, c0, ncol):
            par = ci % 2
            slot = ci % 4
            buf = wch[slot]
            tq = tq_bf[par]
            B.dma("sync", f"w{slot}", buf[:],
                  wsrc[widx][r0:r0 + nr, c0:c0 + ncol],
                  waits=[ld_free[slot]])
            ld = B.at(f"w{slot}")
            B.comp("vector", "vs", lambda e, b=buf, sw=swrep[widx]:
                   e.tensor_scalar(out=b[:], in0=b[:], scalar1=sw[:],
                                   scalar2=-1.0, op0=ALU.mult, op1=ALU.max),
                   waits=[ld], sw=False)
            B.comp("vector", "vs", lambda e, b=buf, p=par: e.tensor_scalar(
                out=fq[p][:], in0=b[:], scalar1=1.0, scalar2=MAGIC,
                op0=ALU.min, op1=ALU.add),
                waits=[ab_free[par]], sw=False)
            ld_free[slot] = B.at("vs")
            B.comp("scalar", "ss", lambda e, t=tq, p=par: e.activation(
                out=t[:], in_=fq[p][:], func=AF.Identity, bias=nmagic_col[:],
                scale=1.0), waits=[wr_done[par], B.at("vs")], sw=False)
            ab_free[par] = B.at("ss")
            B.dma("scalar", f"st{par}",
                  wimg[widx].ap()[r0:r0 + nr, c0:c0 + ncol],
                  tq[:], waits=[B.at("ss")])
            wr_done[par] = B.at(f"st{par}")

        chlist = [wchunks(w) for w in range(3)]
        for w in range(3):
            for ci, ch in enumerate(chlist[w]):
                p1_chunk(w, ci, *ch)
            p1_stats(w)
        ld_free = {0: B.at("vs"), 1: B.at("vs"), 2: B.at("ss"), 3: B.at("ss")}
        for ci in range(64):
            p2_chunk(0, ci, *chlist[0][ci])
        gate_img_done = [B.at("st0"), B.at("st1")]
        for c in range(IC):
            B.dma("sync", "swg", wqTg[:, INNER * c:INNER * (c + 1)],
                  wg_img.ap()[0:INNER, 128 * c:128 * (c + 1)],
                  waits=gate_img_done, transpose=True)
        wqtg_done = B.at("swg")
        for ci in range(64):
            p2_chunk(1, ci, *chlist[1][ci])
        for ci in range(64):
            p2_chunk(2, ci, *chlist[2][ci])
        all_img_done = [B.at("st0"), B.at("st1")]

        if debug:
            B.dma("sync", "dbg", st_dbg.ap()[:, :], wstat[:], waits=[B.at("vs")])
            for i in range(3):
                B.dma("sync", "dbg", sw_dbg.ap()[i:i + 1, :], swrep[i][:],
                      waits=[B.at("vs")])
                B.dma("sync", "dbg", sw_dbg.ap()[3 + i:4 + i, :], sirep[i][:],
                      waits=[B.at("vs")])

        cast_done = B.at("qc")
        xaload = {}
        xbload = {}
        mix_done = {}
        res_done = {}
        hq_wr = {}

        def load_xtiles(k):
            if k >= NT:
                return
            w = []
            if k - 2 >= 0:
                w.append(("ts", mix_done.get(k - 2, 0)))
                w.append(("vs", res_done.get(k - 2, 0)))
            B.dma("sync", f"xa{k % 2}", xf32[k % 2][:],
                  xs[128 * k:128 * (k + 1), :], waits=w)
            xaload[k] = B.at(f"xa{k % 2}")
            B.dma("sync", f"xb{k % 2}", xbf[k % 2][:],
                  xbf_img.ap()[128 * k:128 * (k + 1), :],
                  waits=w + [cast_done])
            xbload[k] = B.at(f"xb{k % 2}")

        ystore = {}
        prev_fm_user = [0]
        wu_busy = {0: 0, 1: 0}
        wd_busy = {0: 0, 1: 0}
        smw_col = {j: B.at(f"sm{j}") for j in range(4)}
        ev_log_prev = [0]
        S = {}

        def emit_xfm(sb):
            t0 = 512 * sb
            fm_w = [cast_done, ("ts", prev_fm_user[0])]
            for c in range(IC):
                B.dma("sync", "sfm", fmbuf[:, 512 * c:512 * (c + 1)],
                      xbf_img.ap()[t0:t0 + 512, 128 * c:128 * (c + 1)],
                      waits=fm_w, transpose=True)
            S["xfm_done"] = B.at("sfm")

        def emit_a1(sb):
            t0 = 512 * sb
            for kk in range(BPS):
                k = BPS * sb + kk
                for c in range(IC):
                    B.comp("tensor", "ts", lambda e, c=c, kk=kk: e.matmul(
                        PB[4][:, 0:K],
                        fmbuf[:, 512 * c + 128 * kk:512 * c + 128 * (kk + 1)],
                        Wt_full[:, 128 * c:128 * c + K],
                        start=(c == 0), stop=(c == IC - 1)),
                        waits=([S["xfm_done"], wqtg_done, wt_done, p0_qx,
                                ("vs", ev_log_prev[0])] if c == 0 else []),
                        inc=(1 if c == IC - 1 else 0))
                t_log = B.at("ts")
                B.comp("vector", "vs", lambda e: e.tensor_add(
                    out=logit_s[:], in0=PB[4][:, 0:K], in1=b_bc[:]), waits=[t_log])
                ev_log_prev[0] = B.cnt["vs"]
                B.comp("vector", "vs", lambda e: e.reduce_max(
                    out=col["amx"][:], in_=logit_s[:], axis=X))
                B.comp("vector", "vs", lambda e: e.tensor_scalar(
                    out=col["negmax"][:], in0=col["amx"][:], scalar1=-1.0,
                    scalar2=None, op0=ALU.mult))
                B.comp("scalar", "ss", lambda e: e.activation(
                    out=expv[:], in_=logit_s[:], func=AF.Exp,
                    bias=col["negmax"][:], scale=1.0, accum_out=col["sumexp"][:]),
                    waits=[B.at("vs")])
                B.comp("vector", "vs", lambda e: e.reciprocal(
                    out=col["rsx"][:], in_=col["sumexp"][:]), waits=[B.at("ss")])
                B.comp("vector", "vs", lambda e, kk=kk: e.tensor_scalar(
                    out=w_sm[:, K * kk:K * (kk + 1)], in0=expv[:],
                    scalar1=col["rsx"][:], scalar2=None,
                    op0=ALU.mult), waits=[smw_col[k % 4]])
                v_sm = B.at("vs")
                img = mimg[k % 4]
                B.dma("gpsimd", f"sm{k % 4}", AP(img, 0, [[MW + 1, 128], [1, K]]),
                      w_sm[:, K * kk:K * (kk + 1)], waits=[v_sm, mread[k - 4]])
                smw_col[k % 4] = B.at(f"sm{k % 4}")

        def emit_a2(sb):
            t0 = 512 * sb
            for kk in range(BPS):
                k = BPS * sb + kk
                B.dma("sync", f"sr{k % 4}", m_tm[:],
                      AP(mimg[k % 4], 0, [[MW, 128], [1, MW]]),
                      waits=[smw_col[k % 4], ("vs", B.cnt["vs"])])
                mread[k] = B.at(f"sr{k % 4}")
                B.comp("vector", "vs", lambda e: e.tensor_copy(
                    out=m_tf[:], in_=m_tm[:]), waits=[mread[k]])
                v_mtf = B.at("vs")
                B.comp("tensor", "ts", lambda e: e.transpose(
                    PB[5][:, 0:128], m_tf[:, 0:128], ident_f[:]),
                    waits=[v_mtf, g_ident])
                B.comp("tensor", "ts", lambda e: e.transpose(
                    PB[5][0:32, 128:256], m_tf[:, 128:MW], ident_f[:]))
                t_mt = B.at("ts")
                B.comp("vector", "vs", lambda e: e.tensor_copy(
                    out=m1t[:], in_=PB[5][:, 0:128]), waits=[t_mt])
                B.comp("vector", "vs", lambda e: e.tensor_copy(
                    out=m2t[:], in_=PB[5][0:32, 128:256]))
                v_mt = B.at("vs")
                for hf in range(2):
                    B.comp("tensor", "ts", lambda e, hf=hf, k=k: e.matmul(
                        PB[hf][:, :], m1t[:], xbf[k % 2][:, 512 * hf:512 * (hf + 1)],
                        start=True, stop=False),
                        waits=([v_mt, xbload[k], xbload[k + 1],
                                ("vs", res_done.get(k - 1, 0))] if hf == 0 else []),
                        inc=0)
                    B.comp("tensor", "ts", lambda e, hf=hf, k=k: e.matmul(
                        PB[hf][:, :], m2t[:],
                        xbf[(k + 1) % 2][0:32, 512 * hf:512 * (hf + 1)],
                        start=False, stop=True), inc=1)
                mix_done[k] = B.cnt["ts"]
                t_mix = B.at("ts")
                for hf in range(2):
                    B.comp("vector", "vs", lambda e, hf=hf, k=k: e.tensor_add(
                        out=r_t[:, 512 * hf:512 * (hf + 1)],
                        in0=xf32[k % 2][:, 512 * hf:512 * (hf + 1)],
                        in1=PB[hf][:, :]),
                        waits=([t_mix, xaload[k]] if hf == 0 else []))
                res_done[k] = B.cnt["vs"]
                load_xtiles(k + 2)
                B.comp("vector", "vs", lambda e: e.reduce_sum(
                    out=col["sum"][:], in_=r_t[:], axis=X))
                B.comp("vector", "vs", lambda e: e.tensor_scalar(
                    out=col["mean"][:], in0=col["sum"][:], scalar1=float(2.0 ** -10),
                    scalar2=None, op0=ALU.mult))
                B.comp("scalar", "ss", lambda e: e.activation(
                    out=u_scr[:], in_=r_t[:], func=AF.Square, bias=0.0, scale=1.0,
                    accum_out=col["sq"][:]), waits=[B.at("vs")])
                B.comp("vector", "vs", lambda e: e.tensor_scalar(
                    out=col["msq"][:], in0=col["sq"][:], scalar1=float(2.0 ** -10),
                    scalar2=None, op0=ALU.mult), waits=[B.at("ss")])
                B.comp("vector", "vs", lambda e: e.tensor_mul(
                    out=col["m2c"][:], in0=col["mean"][:], in1=col["mean"][:]))
                B.comp("vector", "vs", lambda e: e.tensor_sub(
                    out=col["var"][:], in0=col["msq"][:], in1=col["m2c"][:]))
                B.comp("scalar", "ss", lambda e: e.activation(
                    out=col["stdv"][:], in_=col["var"][:], func=AF.Sqrt,
                    bias=eps_col[:], scale=1.0), waits=[B.at("vs")])
                B.comp("vector", "vs", lambda e: e.reciprocal(
                    out=col["rstd"][:], in_=col["stdv"][:]), waits=[B.at("ss")])
                B.comp("vector", "vs", lambda e, kk=kk: e.tensor_scalar(
                    out=h_sb[:, D * kk:D * (kk + 1)], in0=r_t[:],
                    scalar1=col["mean"][:], scalar2=col["rstd"][:],
                    op0=ALU.subtract, op1=ALU.mult))
                v_h = B.at("vs")
                if debug:
                    B.dma("sync", "dbg", h_dbg.ap()[128 * k:128 * (k + 1), :],
                          h_sb[:, D * kk:D * (kk + 1)], waits=[v_h])
                    B.comp("vector", "vs", lambda e, kk=kk: e.tensor_copy(
                        out=u_scr[:, 0:K], in_=w_sm[:, K * kk:K * (kk + 1)]))
                    B.dma("sync", "dbg", sm_dbg.ap()[128 * k:128 * (k + 1), :],
                          u_scr[:, 0:K], waits=[B.at("vs")])
                B.comp("vector", "vs", lambda e, kk=kk: e.reduce_max(
                    out=col["amx"][:], in_=h_sb[:, D * kk:D * (kk + 1)], axis=X,
                    apply_absolute_value=True))
                B.comp("vector", "vs", lambda e: e.tensor_scalar(
                    out=col["t0"][:], in0=col["amx"][:], scalar1=1e-5, scalar2=None,
                    op0=ALU.max))
                B.comp("vector", "vs", lambda e: e.reciprocal(
                    out=col["t1"][:], in_=col["t0"][:]))
                B.comp("vector", "vs", lambda e: e.tensor_scalar(
                    out=col["s_col"][:], in0=col["t1"][:], scalar1=127.0,
                    scalar2=None, op0=ALU.mult))
                B.comp("vector", "vs", lambda e, kk=kk: e.reciprocal(
                    out=sinv_sb[:, kk:kk + 1], in_=col["s_col"][:]))
                B.comp("vector", "vs", lambda e, kk=kk: e.tensor_scalar(
                    out=u_scr[:], in0=h_sb[:, D * kk:D * (kk + 1)],
                    scalar1=col["s_col"][:], scalar2=MAGIC,
                    op0=ALU.mult, op1=ALU.add))
                B.comp("vector", "vs", lambda e: e.tensor_scalar(
                    out=hq_t[:], in0=u_scr[:], scalar1=MAGIC, scalar2=None,
                    op0=ALU.subtract),
                    waits=([hq_wr[k - 1]] if k >= 1 else []))
                v_hq = B.at("vs")
                B.dma("sync", "shq", hq_img.ap()[128 * k:128 * (k + 1), :],
                      hq_t[:], waits=[v_hq])
                hq_wr[k] = B.at("shq")
                B.comp("vector", "vs", lambda e, kk=kk, w=sirep[0]: e.tensor_scalar(
                    out=cg_sb[:, kk:kk + 1], in0=sinv_sb[:, kk:kk + 1],
                    scalar1=w[:], scalar2=None, op0=ALU.mult))
                B.comp("vector", "vs", lambda e, kk=kk, w=sirep[1]: e.tensor_scalar(
                    out=cu_sb[:, kk:kk + 1], in0=sinv_sb[:, kk:kk + 1],
                    scalar1=w[:], scalar2=None, op0=ALU.mult))
                B.comp("vector", "vs", lambda e, kk=kk: e.tensor_scalar(
                    out=bb_scr[:], in0=ones_f[:], scalar1=cg_sb[:, kk:kk + 1],
                    scalar2=None, op0=ALU.mult))
                B.comp("tensor", "ts", lambda e: e.transpose(
                    PB[6][:, 0:128], bb_scr[:], ident_f[:]),
                    waits=[B.at("vs"), ("vs", B.cnt["vs"])])
                B.comp("vector", "vs", lambda e, kk=kk: e.tensor_copy(
                    out=cgb[:, 128 * kk:128 * (kk + 1)], in_=PB[6][:, 0:128]),
                    waits=[B.at("ts")])

        def emit_b1(sb):
            t0 = 512 * sb
            a_done = B.at("vs")
            for c in range(IC):
                B.dma("sync", "sfm", fmbuf[:, 512 * c:512 * (c + 1)],
                      hq_img.ap()[t0:t0 + 512, 128 * c:128 * (c + 1)],
                      waits=[hq_wr[BPS * sb + BPS - 1],
                             ("ts", B.cnt["ts"])], transpose=True)
            hqt_done = B.at("sfm")
            prev_fm_user[0] = 0

            B.comp("vector", "vs", lambda e: e.memset(maxac[:], 0.0))
            B.comp("vector", "vs", lambda e: e.memset(minac[:], 0.0))
            v_am0 = B.at("vs")

            g_evac = {0: a_done, 1: a_done}
            u_evac = {0: B.at("vs"), 1: B.at("vs")}
            for ob in range(OB):
                wub = wuT[ob % 2]
                ww = all_img_done + [("ts", wu_busy[ob % 2])]
                for c in range(IC):
                    B.dma("sync", f"su{ob % 2}", wub[:, 512 * c:512 * (c + 1)],
                          wu_img.ap()[512 * ob:512 * (ob + 1),
                                      128 * c:128 * (c + 1)],
                          waits=ww if c == 0 else [], transpose=True)
                wu_ld = B.at(f"su{ob % 2}")
                for oc in range(4):
                    g = 4 * ob + oc
                    pg, pu = PB[g % 2], PB[2 + g % 2]
                    for c in range(IC):
                        B.comp("tensor", "ts", lambda e, c=c, g=g, pg=pg: e.matmul(
                            pg[:, :],
                            wqTg[:, INNER * c + 128 * g:INNER * c + 128 * (g + 1)],
                            fmbuf[:, 512 * c:512 * (c + 1)],
                            start=(c == 0), stop=(c == IC - 1)),
                            waits=([hqt_done, wqtg_done, g_evac[g % 2]]
                                   if c == 0 else []),
                            inc=(1 if c == IC - 1 else 0))
                    t_g = B.at("ts")
                    for c in range(IC):
                        B.comp("tensor", "ts", lambda e, c=c, oc=oc, pu=pu, wub=wub:
                               e.matmul(
                            pu[:, :],
                            wub[:, 512 * c + 128 * oc:512 * c + 128 * (oc + 1)],
                            fmbuf[:, 512 * c:512 * (c + 1)],
                            start=(c == 0), stop=(c == IC - 1)),
                            waits=([wu_ld, u_evac[g % 2]] if c == 0 else []),
                            inc=(1 if c == IC - 1 else 0))
                    t_u = B.at("ts")
                    wu_busy[ob % 2] = B.cnt["ts"]
                    prev_fm_user[0] = B.cnt["ts"]
                    B.comp("vector", "vs", lambda e, pg=pg: e.tensor_mul(
                        out=gated[:], in0=pg[:, :], in1=cgb[:]), waits=[t_g],
                        sw=True)
                    g_evac[g % 2] = B.at("vs")
                    B.comp("scalar", "ss", lambda e: e.activation(
                        out=silu_g[:], in_=gated[:], func=AF.Silu, bias=0.0,
                        scale=1.0), waits=[B.at("vs")], sw=False)
                    B.comp("vector", "vs", lambda e, pu=pu, g=g: e.tensor_mul(
                        out=P_sb[:, 512 * g:512 * (g + 1)], in0=pu[:, :],
                        in1=silu_g[:]), waits=[t_u, B.at("ss")], sw=True)
                    u_evac[g % 2] = B.at("vs")
                    B.comp("vector", "vs", lambda e, g=g: e.tensor_tensor(
                        out=maxac[:], in0=maxac[:],
                        in1=P_sb[:, 512 * g:512 * (g + 1)], op=ALU.max),
                        waits=[v_am0])
                    B.comp("vector", "vs", lambda e, g=g: e.tensor_tensor(
                        out=minac[:], in0=minac[:],
                        in1=P_sb[:, 512 * g:512 * (g + 1)], op=ALU.min))
            if debug:
                B.comp("vector", "vs", lambda e, sb=sb: e.tensor_copy(
                    out=gated[:], in_=P_sb[:, 0:512]))
                B.dma("sync", "dbg", P_dbg.ap()[sb, :, 0:512], gated[:],
                      waits=[B.at("vs")])
            B.comp("vector", "vs", lambda e: e.tensor_scalar(
                out=amax_acc[:], in0=minac[:], scalar1=-1.0, scalar2=None,
                op0=ALU.mult))
            B.comp("vector", "vs", lambda e: e.tensor_tensor(
                out=amax_acc[:], in0=amax_acc[:], in1=maxac[:], op=ALU.max))
            S["g_amax"] = B.at("vs")

        def emit_stats(sb):
            t0 = 512 * sb
            for tb in range(BPS):
                B.comp("tensor", "ts", lambda e, tb=tb: e.transpose(
                    PB[6 + tb % 2][:, 0:128], amax_acc[:, 128 * tb:128 * (tb + 1)],
                    ident_f[:]), waits=[S["g_amax"], ("vs", B.cnt["vs"])])
                B.comp("vector", "vs", lambda e, tb=tb: e.reduce_max(
                    out=amp_sb[:, tb:tb + 1], in_=PB[6 + tb % 2][:, 0:128], axis=X),
                    waits=[B.at("ts")])
                B.comp("vector", "vs", lambda e, tb=tb: e.tensor_scalar(
                    out=col["t0"][:], in0=amp_sb[:, tb:tb + 1],
                    scalar1=cu_sb[:, tb:tb + 1], scalar2=1e-5,
                    op0=ALU.mult, op1=ALU.max))
                B.comp("vector", "vs", lambda e: e.reciprocal(
                    out=col["t1"][:], in_=col["t0"][:]))
                B.comp("vector", "vs", lambda e, tb=tb: e.tensor_scalar(
                    out=s2_sb[:, tb:tb + 1], in0=col["t1"][:], scalar1=127.0,
                    scalar2=None, op0=ALU.mult))
                B.comp("vector", "vs", lambda e, tb=tb: e.reciprocal(
                    out=s2i_sb[:, tb:tb + 1], in_=s2_sb[:, tb:tb + 1]))
                B.comp("vector", "vs", lambda e, tb=tb: e.tensor_mul(
                    out=qm_sb[:, tb:tb + 1], in0=cu_sb[:, tb:tb + 1],
                    in1=s2_sb[:, tb:tb + 1]))
                B.comp("vector", "vs", lambda e, tb=tb, w=sirep[2]: e.tensor_scalar(
                    out=cd_sb[:, tb:tb + 1], in0=s2i_sb[:, tb:tb + 1],
                    scalar1=w[:], scalar2=None, op0=ALU.mult))
                B.comp("vector", "vs", lambda e, tb=tb: e.tensor_scalar(
                    out=bb_scr[:], in0=ones_f[:], scalar1=qm_sb[:, tb:tb + 1],
                    scalar2=None, op0=ALU.mult))
                B.comp("tensor", "ts", lambda e, tb=tb: e.transpose(
                    PB[6 + tb % 2][:, 0:128], bb_scr[:], ident_f[:]),
                    waits=[("vs", B.cnt["vs"])])
                B.comp("vector", "vs", lambda e, tb=tb: e.tensor_copy(
                    out=qmb[:, 128 * tb:128 * (tb + 1)],
                    in_=PB[6 + tb % 2][:, 0:128]), waits=[B.at("ts")])
            S["v_qmb"] = B.at("vs")

        def emit_down(sb):
            t0 = 512 * sb
            v_psum_free = ("vs", B.cnt["vs"])
            uq = [gated, u1_s]
            iq_ct = 0
            for g in range(OC):
                wb = wdT[g % 2]
                B.dma("scalar", f"sd{g % 2}", wb[:],
                      wd_img.ap()[0:D, 128 * g:128 * (g + 1)],
                      waits=all_img_done + [("ts", wd_busy[g % 2])],
                      transpose=True)
                wd_ld = B.at(f"sd{g % 2}")
                B.comp("gpsimd", "gs", lambda e, g=g, u=uq[g % 2]: e.tensor_mul(
                    out=u[:], in0=P_sb[:, 512 * g:512 * (g + 1)], in1=qmb[:]),
                    waits=[S["v_qmb"]], sw=False)
                B.comp("vector", "vs", lambda e, g=g, u=uq[g % 2]: e.tensor_scalar(
                    out=P_sb[:, 512 * g:512 * (g + 1)], in0=u[:], scalar1=MAGIC,
                    scalar2=MAGIC, op0=ALU.add, op1=ALU.subtract),
                    waits=[B.at("gs")], sw=True)
                q3_at = B.at("vs")
                if debug and g == 0:
                    B.comp("vector", "vs", lambda e, sb=sb: e.tensor_copy(
                        out=bb_scr[:], in_=P_sb[:, 0:128]), waits=[q3_at])
                    B.dma("sync", "dbg", iq_dbg.ap()[sb, :, 0:128], bb_scr[:],
                          waits=[B.at("vs")])
                for tb in range(BPS):
                    for jh in range(2):
                        B.comp("tensor", "ts", lambda e, g=g, tb=tb, jh=jh, wb=wb:
                               e.matmul(
                            PB[2 * tb + jh][:, :],
                            P_sb[:, 512 * g + 128 * tb:512 * g + 128 * (tb + 1)],
                            wb[:, 512 * jh:512 * (jh + 1)],
                            start=(g == 0), stop=(g == OC - 1)),
                            waits=([wd_ld, q3_at, v_psum_free]
                                   if (tb == 0 and jh == 0) else []),
                            inc=(1 if (tb == BPS - 1 and jh == 1) else 0))
                wd_busy[g % 2] = B.cnt["ts"]
            S["t_down"] = B.at("ts")

        def emit_evac(sb):
            t0 = 512 * sb
            for tb in range(BPS):
                for jh in range(2):
                    B.comp("vector", "vs", lambda e, tb=tb, jh=jh:
                           e.scalar_tensor_tensor(
                        out=r_t[:, 512 * jh:512 * (jh + 1)],
                        in0=PB[2 * tb + jh][:, :], scalar=cd_sb[:, tb:tb + 1],
                        in1=h_sb[:, D * tb + 512 * jh:D * tb + 512 * (jh + 1)],
                        op0=ALU.mult, op1=ALU.add),
                        waits=([S["t_down"]] if (tb == 0 and jh == 0) else []))
                B.comp("vector", "vs", lambda e: e.reduce_sum(
                    out=col["sum"][:], in_=r_t[:], axis=X))
                B.comp("vector", "vs", lambda e: e.tensor_scalar(
                    out=col["mean"][:], in0=col["sum"][:], scalar1=float(2.0 ** -10),
                    scalar2=None, op0=ALU.mult))
                B.comp("scalar", "ss", lambda e: e.activation(
                    out=u_scr[:], in_=r_t[:], func=AF.Square, bias=0.0, scale=1.0,
                    accum_out=col["sq"][:]), waits=[B.at("vs")])
                B.comp("vector", "vs", lambda e: e.tensor_scalar(
                    out=col["msq"][:], in0=col["sq"][:], scalar1=float(2.0 ** -10),
                    scalar2=None, op0=ALU.mult), waits=[B.at("ss")])
                B.comp("vector", "vs", lambda e: e.tensor_mul(
                    out=col["m2c"][:], in0=col["mean"][:], in1=col["mean"][:]))
                B.comp("vector", "vs", lambda e: e.tensor_sub(
                    out=col["var"][:], in0=col["msq"][:], in1=col["m2c"][:]))
                B.comp("scalar", "ss", lambda e: e.activation(
                    out=col["stdv"][:], in_=col["var"][:], func=AF.Sqrt,
                    bias=eps_col[:], scale=1.0), waits=[B.at("vs")])
                B.comp("vector", "vs", lambda e: e.reciprocal(
                    out=col["rstd"][:], in_=col["stdv"][:]), waits=[B.at("ss")])
                ysl = (BPS * sb + tb) % 2
                yt = ytile[ysl]
                B.comp("vector", "vs", lambda e, yt=yt: e.tensor_scalar(
                    out=yt[:], in0=r_t[:], scalar1=col["mean"][:],
                    scalar2=col["rstd"][:], op0=ALU.subtract, op1=ALU.mult),
                    waits=([ystore[ysl]] if ysl in ystore else []))
                B.dma("sync", f"sy{ysl}", yo[t0 + 128 * tb:t0 + 128 * (tb + 1), :],
                      yt[:], waits=[B.at("vs")])
                ystore[ysl] = B.at(f"sy{ysl}")


        load_xtiles(0)
        load_xtiles(1)
        for sb in range(NSB):
            emit_xfm(sb)
            emit_a1(sb)
            emit_a2(sb)
            emit_b1(sb)
            emit_stats(sb)
            emit_down(sb)
            emit_evac(sb)
        B.ev("sync", None, waits=[B.at("sy0"), B.at("sy1")])

        with nc.Block() as block:
            def run_engine(ename, engine):
                for eng, fn, waits, incs in B.events:
                    if eng != ename:
                        continue
                    for sname, val in waits:
                        if val > 0:
                            engine.wait_ge(sems[sname], val)
                    if fn is None:
                        continue
                    inst = fn(engine)
                    for sname, n in incs:
                        inst = inst.then_inc(sems[sname], n)

            block.sync(lambda e: run_engine("sync", e))
            block.vector(lambda e: run_engine("vector", e))
            block.scalar(lambda e: run_engine("scalar", e))
            block.tensor(lambda e: run_engine("tensor", e))
            block.gpsimd(lambda e: run_engine("gpsimd", e))

    return nc


_NC_CACHE = [None]


def kernel(L0_attn_x, L0_attn_W, L0_attn_b, L0_ffn_w_gate, L0_ffn_w_up,
           L0_ffn_w_down):
    x = np.ascontiguousarray(np.asarray(L0_attn_x, dtype=np.float32))
    Wm = np.ascontiguousarray(np.asarray(L0_attn_W, dtype=np.float32))
    bv = np.asarray(L0_attn_b, dtype=np.float32)
    wg = np.ascontiguousarray(np.asarray(L0_ffn_w_gate, dtype=np.float32))
    wu = np.ascontiguousarray(np.asarray(L0_ffn_w_up, dtype=np.float32))
    wd = np.ascontiguousarray(np.asarray(L0_ffn_w_down, dtype=np.float32))
    Bn, Ln, Dn = x.shape
    assert (Bn, Ln, Dn) == (2, 8192, 1024)

    bb = np.ascontiguousarray(np.broadcast_to(bv[None, :], (128, K)))
    in_maps = []
    for core in range(8):
        b, q = divmod(core, 4)
        t0 = q * T
        shard = np.zeros((TP, D), dtype=np.float32)
        end = min(Ln, t0 + TP)
        shard[0:end - t0] = x[b, t0:end]
        in_maps.append({"xs": shard, "Wm": Wm, "bb": bb,
                        "wg": wg, "wu": wu, "wd": wd})

    if _NC_CACHE[0] is None:
        _NC_CACHE[0] = build_core_program()
    nc = _NC_CACHE[0]
    res = run_bass_kernel_spmd(nc, in_maps, core_ids=list(range(8)))
    out = np.empty((2, 8192, 1024), dtype=np.float32)
    for core in range(8):
        b, q = divmod(core, 4)
        out[b, q * T:(q + 1) * T] = res.results[core]["yo"]
    return out


if __name__ == "__main__":
    pass


# version 27
# speedup vs baseline: 1.1591x; 1.1591x over previous
import contextlib
import numpy as np
import concourse.bass as bass
import concourse.mybir as mybir
from concourse.ap import AP
from concourse.bass_utils import run_bass_kernel_spmd

F32 = mybir.dt.float32
BF16 = mybir.dt.bfloat16
AF = mybir.ActivationFunctionType
ALU = mybir.AluOpType
X = mybir.AxisListType.X

MAGIC = 12582912.0
D = 1024
K = 32
INNER = 4096
T = 2048
TP = 2176
NT = TP // 128
NBLK = 16
NSB = 4
BPS = 4
IC = 8
OC = 32
OB = 8
MW = 160


class Builder:

    def __init__(self):
        self.events = []
        self.cnt = {}

    def sem_names(self):
        return sorted(self.cnt)

    def ev(self, eng, fn, waits=(), incs=()):
        w = {}
        for name, val in waits:
            w[name] = max(w.get(name, 0), val)
        for name, n in incs:
            self.cnt[name] = self.cnt.get(name, 0) + n
        self.events.append((eng, fn, sorted(w.items()), list(incs)))

    def dma(self, eng, sem, out, in_, waits=(), transpose=False):
        self.cnt.setdefault(sem, 0)
        if transpose:
            fn = lambda e, o=out, i=in_: e.dma_start_transpose(o, i)
        else:
            fn = lambda e, o=out, i=in_: e.dma_start(out=o, in_=i)
        self.ev(eng, fn, waits, incs=[(sem, 16)])
        return self.cnt[sem]

    def comp(self, eng, sem, fn, waits=(), inc=1, sw=True):
        self.cnt.setdefault(sem, 0)
        waits = list(waits)
        if sw and eng in ("vector", "scalar", "gpsimd"):
            waits.append((sem, self.cnt[sem]))
        self.ev(eng, fn, waits, incs=([(sem, inc)] if inc else []))
        return self.cnt[sem]

    def at(self, sem):
        return (sem, self.cnt.get(sem, 0))


def build_core_program(debug=False):
    nc = bass.Bass("TRN2", target_bir_lowering=False)
    ikind = "ExternalOutput" if debug else "Internal"

    xs = nc.dram_tensor("xs", [TP, D], F32, kind="ExternalInput").ap()
    Wm = nc.dram_tensor("Wm", [K, D], F32, kind="ExternalInput").ap()
    bb = nc.dram_tensor("bb", [128, K], F32, kind="ExternalInput").ap()
    wg = nc.dram_tensor("wg", [INNER, D], F32, kind="ExternalInput").ap()
    wu = nc.dram_tensor("wu", [INNER, D], F32, kind="ExternalInput").ap()
    wd = nc.dram_tensor("wd", [D, INNER], F32, kind="ExternalInput").ap()
    yo = nc.dram_tensor("yo", [T, D], F32, kind="ExternalOutput").ap()

    xbf_img = nc.dram_tensor("xbf_img", [TP, D], BF16, kind=ikind)
    hq_img = nc.dram_tensor("hq_img", [T, D], BF16, kind=ikind)
    wg_img = nc.dram_tensor("wg_img", [INNER, D], BF16, kind=ikind)
    wu_img = nc.dram_tensor("wu_img", [INNER, D], BF16, kind=ikind)
    wd_img = nc.dram_tensor("wd_img", [D, INNER], BF16, kind=ikind)
    wm_img = nc.dram_tensor("wm_img", [128, D], BF16, kind="Internal")
    mimg = [nc.dram_tensor(f"mimg{j}", [(MW + 1) * 128], BF16, kind="Internal")
            for j in range(4)]
    h_dbg = nc.dram_tensor("h_dbg", [T, D], F32, kind=ikind) if debug else None
    sm_dbg = nc.dram_tensor("sm_dbg", [T, K], F32, kind=ikind) if debug else None
    P_dbg = nc.dram_tensor("P_dbg", [NSB, 128, OC * 512], F32, kind=ikind) if debug else None
    st_dbg = nc.dram_tensor("st_dbg", [1, 32], F32, kind=ikind) if debug else None
    sw_dbg = nc.dram_tensor("sw_dbg", [6, 128], F32, kind=ikind) if debug else None
    rc_dbg = nc.dram_tensor("rc_dbg", [3, 128, 64], F32, kind=ikind) if debug else None
    iq_dbg = nc.dram_tensor("iq_dbg", [NSB, 128, OC * 512], F32, kind=ikind) if debug else None

    B = Builder()
    es = contextlib.ExitStack()
    with es:
        def sb_(name, shape, dt):
            return es.enter_context(nc.sbuf_tensor(name, shape, dt))

        ones_f = sb_("ones_f", [128, 128], F32)
        ident_f = sb_("ident_f", [128, 128], F32)
        onescol = sb_("onescol", [128, 1], F32)
        b_bc = sb_("b_bc", [128, K], F32)
        eps_col = sb_("eps_col", [128, 1], F32)
        nhalf_col = sb_("nhalf_col", [128, 1], F32)
        phalf_col = sb_("phalf_col", [128, 1], F32)
        magic_col = sb_("magic_col", [128, 1], F32)
        nmagic_col = sb_("nmagic_col", [128, 1], F32)
        zeros_m = sb_("zeros_m", [128, MW + 1], BF16)
        Wt_full = sb_("Wt_full", [128, IC * 128], BF16)
        swrep = [sb_(f"swrep{i}", [128, 1], F32) for i in range(3)]
        sirep = [sb_(f"sirep{i}", [128, 1], F32) for i in range(3)]
        wstat = sb_("wstat", [1, 32], F32)

        wch = [sb_(f"wch{j}", [128, 512], F32) for j in range(4)]
        fq = [sb_(f"fq{j}", [128, 512], F32) for j in range(2)]
        tq_bf = [sb_(f"tq_bf{j}", [128, 512], BF16) for j in range(2)]
        rcols = sb_("rcols", [128, 64], F32)
        abs_dump = sb_("abs_dump", [128, 512], F32)
        r8 = sb_("r8", [128, 8], F32)
        pc_col = sb_("pc_col", [128, 1], F32)
        ones8 = sb_("ones8", [128, 8], F32)
        W_f = [wch[0][0:K, :], wch[1][0:K, :]]
        W_bt = sb_("W_bt", [K, D], BF16)

        wqTg = sb_("wqTg", [128, IC * INNER], BF16)
        wuT = [sb_(f"wuT{j}", [128, IC * 512], BF16) for j in range(2)]
        wdT = [sb_(f"wdT{j}", [128, D], BF16) for j in range(2)]
        P_sb = sb_("P_sb", [128, OC * 512], BF16)
        h_sb = sb_("h_sb", [128, BPS * D], F32)
        fmbuf = sb_("fmbuf", [128, IC * 512], BF16)

        xf32 = [sb_(f"xf32_{j}", [128, D], F32) for j in range(2)]
        xbf = [sb_(f"xbf_{j}", [128, D], BF16) for j in range(2)]
        ytile = [sb_(f"ytile{j}", [128, D], F32) for j in range(2)]
        r_t = sb_("r_t", [128, D], F32)
        u_scr = sb_("u_scr", [128, D], F32)
        hq_t = sb_("hq_t", [128, D], BF16)
        gated = sb_("gated", [128, 512], F32)
        silu_g = sb_("silu_g", [128, 512], F32)
        amax_acc = sb_("amax_acc", [128, 512], F32)
        maxac = sb_("maxac", [128, 512], BF16)
        minac = sb_("minac", [128, 512], BF16)
        u1_s = silu_g
        cgb = sb_("cgb", [128, 512], F32)
        qmb = sb_("qmb", [128, 512], F32)
        bb_scr = sb_("bb_scr", [128, 128], F32)
        m_tm = sb_("m_tm", [128, MW], BF16)
        m_tf = sb_("m_tf", [128, MW], F32)
        m1t = sb_("m1t", [128, 128], BF16)
        m2t = sb_("m2t", [32, 128], BF16)
        w_sm = sb_("w_sm", [128, BPS * K], BF16)
        logit_s = sb_("logit_s", [128, K], F32)
        expv = sb_("expv", [128, K], F32)
        col = {name: sb_(f"c_{name}", [128, 1], F32) for name in
               ["sum", "mean", "sq", "msq", "m2c", "var", "stdv", "rstd",
                "amx", "t0", "t1", "s_col", "negmax", "sumexp", "rsx"]}
        sinv_sb = sb_("sinv_sb", [128, BPS], F32)
        cg_sb = sb_("cg_sb", [128, BPS], F32)
        cu_sb = sb_("cu_sb", [128, BPS], F32)
        cd_sb = sb_("cd_sb", [128, BPS], F32)
        amp_sb = sb_("amp_sb", [128, BPS], F32)
        s2_sb = sb_("s2_sb", [128, BPS], F32)
        s2i_sb = sb_("s2i_sb", [128, BPS], F32)
        qm_sb = sb_("qm_sb", [128, BPS], F32)

        PB = [es.enter_context(nc.psum_tensor(f"PB{j}", [128, 512], F32))
              for j in range(8)]

        sems = {}
        semnames = (["qc", "zw", "swt", "swg", "sc0", "sfm", "shq", "dbg",
                     "ts", "vs", "ss", "gs"]
                    + [f"w{j}" for j in range(4)]
                    + ["st0", "st1", "sy0", "sy1", "xa0", "xa1", "xb0", "xb1",
                       "su0", "su1", "sd0", "sd1"]
                    + [f"sm{j}" for j in range(4)] + [f"sr{j}" for j in range(4)])
        for s in semnames:
            sems[s] = es.enter_context(nc.semaphore(s))
            B.cnt[s] = 0

        B.comp("vector", "vs", lambda e: e.memset(ones_f[:], 1.0))
        B.comp("vector", "vs", lambda e: e.memset(onescol[:], 1.0))
        B.comp("vector", "vs", lambda e: e.memset(eps_col[:], 1e-5))
        B.comp("vector", "vs", lambda e: e.memset(nhalf_col[:], -0.5))
        B.comp("vector", "vs", lambda e: e.memset(phalf_col[:], 0.5))
        B.comp("vector", "vs", lambda e: e.memset(magic_col[:], MAGIC))
        B.comp("vector", "vs", lambda e: e.memset(nmagic_col[:], -MAGIC))
        B.comp("vector", "vs", lambda e: e.memset(zeros_m[:], 0.0))
        B.comp("vector", "vs", lambda e: e.memset(tq_bf[0][:], 0.0))
        B.comp("vector", "vs", lambda e: e.memset(tq_bf[1][:], 0.0))
        v_consts = B.at("vs")
        B.comp("gpsimd", "gs", lambda e: e.affine_select(
            ident_f[:], ones_f[:], pattern=[[-1, 128]], compare_op=ALU.is_equal,
            fill=0.0, base=0, channel_multiplier=1), waits=[v_consts])
        B.comp("gpsimd", "gs", lambda e: e.affine_select(
            ones8[:], ones_f[:, 0:8], pattern=[[-16, 8]], compare_op=ALU.is_ge,
            fill=0.0, base=0, channel_multiplier=1), waits=[v_consts])
        B.comp("gpsimd", "gs", lambda e: e.affine_select(
            ones8[:], ones8[:], pattern=[[16, 8]], compare_op=ALU.is_ge,
            fill=0.0, base=15, channel_multiplier=-1))
        g_ident = B.at("gs")
        B.dma("gpsimd", "qc", xbf_img.ap(), xs[:, :])
        mread = {}
        for j in range(4):
            B.dma("gpsimd", f"sr{j}", AP(mimg[j], 0, [[1, (MW + 1) * 128]]),
                  zeros_m[:], waits=[v_consts])
            mread[j - 4] = B.at(f"sr{j}")
        B.dma("sync", "sc0", W_f[0], Wm[:, 0:512])
        B.dma("sync", "sc0", W_f[1], Wm[:, 512:1024])
        B.comp("vector", "vs", lambda e: e.tensor_copy(out=W_bt[:, 0:512],
               in_=W_f[0]), waits=[B.at("sc0")])
        B.comp("vector", "vs", lambda e: e.tensor_copy(out=W_bt[:, 512:1024],
               in_=W_f[1]))
        B.dma("sync", "zw", wm_img.ap()[0:128, 0:512], tq_bf[0][:], waits=[v_consts])
        B.dma("sync", "zw", wm_img.ap()[0:128, 512:1024], tq_bf[1][:], waits=[v_consts])
        B.dma("sync", "zw", wm_img.ap()[0:K, :], W_bt[:], waits=[B.at("vs"), B.at("zw")])
        w_img_done = B.at("zw")
        for c in range(IC):
            B.dma("sync", "swt", Wt_full[:, 128 * c:128 * (c + 1)],
                  wm_img.ap()[0:128, 128 * c:128 * (c + 1)],
                  waits=[w_img_done], transpose=True)
        wt_done = B.at("swt")
        B.dma("sync", "sc0", b_bc[:], bb[:, :])
        p0_qx = B.at("sc0")

        wsrc = [wg, wu, wd]
        wimg = [wg_img, wu_img, wd_img]

        def wchunks(widx):
            if widx < 2:
                return [(128 * c, 128, 512 * s, 512) for c in range(32)
                        for s in range(2)]
            return [(128 * r, 128, 512 * s, 512) for r in range(8) for s in range(8)]

        p1s = {"done": {j: B.at("vs") for j in range(4)}, "prev": B.at("vs")}

        def p1_chunk(widx, ci, r0, nr, c0, ncol):
            slot = ci % 4
            B.dma("sync", f"w{slot}" if slot < 4 else "w0", wch[slot][:],
                  wsrc[widx][r0:r0 + nr, c0:c0 + ncol],
                  waits=[p1s["done"][slot]])
            ld = B.at(f"w{slot}")
            if slot < 2:
                B.comp("vector", "vs", lambda e, ci=ci, slot=slot: e.reduce_sum(
                    out=rcols[:, ci:ci + 1], in_=wch[slot][:], axis=X,
                    apply_absolute_value=True), waits=[ld, p1s["prev"]],
                    sw=False)
                p1s["done"][slot] = B.at("vs")
            else:
                B.comp("scalar", "ss", lambda e, ci=ci, slot=slot: e.activation(
                    out=abs_dump[:], in_=wch[slot][:], func=AF.Abs, bias=0.0,
                    scale=1.0, accum_out=rcols[:, ci:ci + 1]),
                    waits=[ld, p1s["prev"]])
                p1s["done"][slot] = B.at("ss")

        def p1_stats(widx):
            B.comp("vector", "vs", lambda e: e.reduce_sum(
                out=r8[:].rearrange("p (a b) -> p a b", b=1),
                in_=rcols[:].rearrange("p (a b) -> p a b", b=8), axis=X),
                waits=[B.at("ss")])
            B.comp("vector", "vs", lambda e: e.reduce_sum(
                out=pc_col[:], in_=r8[:], axis=X))
            v_acc = B.at("vs")
            if debug:
                B.dma("sync", "dbg", rc_dbg.ap()[widx, :, :], rcols[:],
                      waits=[v_acc])
            B.comp("tensor", "ts", lambda e: e.matmul(
                PB[7][0:1, 0:8], pc_col[:], ones8[:], start=True, stop=True),
                waits=[v_acc, g_ident, B.at("vs")])
            B.comp("vector", "vs", lambda e: e.tensor_copy(
                out=wstat[0:1, 16:24], in_=PB[7][0:1, 0:8]), waits=[B.at("ts")])
            B.comp("vector", "vs", lambda e: e.reduce_sum(
                out=wstat[0:1, 15:16], in_=wstat[0:1, 16:24], axis=X))
            w8 = widx * 5
            B.comp("vector", "vs", lambda e, o=wstat[0:1, w8:w8 + 1]:
                   e.tensor_scalar(
                out=o, in0=wstat[0:1, 15:16], scalar1=float(2.0 ** -22),
                scalar2=None, op0=ALU.mult))
            B.comp("vector", "vs", lambda e, i=wstat[0:1, w8:w8 + 1],
                   o=wstat[0:1, w8 + 1:w8 + 2]: e.tensor_scalar(
                out=o, in0=i, scalar1=1e-5, scalar2=None, op0=ALU.max))
            B.comp("vector", "vs", lambda e, i=wstat[0:1, w8 + 1:w8 + 2],
                   o=wstat[0:1, w8 + 2:w8 + 3]: e.reciprocal(out=o, in_=i))
            B.comp("vector", "vs", lambda e, i=wstat[0:1, w8 + 2:w8 + 3],
                   o=wstat[0:1, w8 + 3:w8 + 4]: e.reciprocal(out=o, in_=i))
            v_stats = B.at("vs")
            for which, dst in ((2, swrep[widx]), (3, sirep[widx])):
                B.comp("tensor", "ts",
                       lambda e, wv=wstat[0:1, w8 + which:w8 + which + 1]:
                       e.matmul(PB[7][:, 0:1], ones_f[0:1, 0:128], wv,
                                start=True, stop=True),
                       waits=[v_stats, B.at("vs")])
                B.comp("vector", "vs", lambda e, d=dst: e.tensor_copy(
                    out=d[:], in_=PB[7][:, 0:1]), waits=[B.at("ts")])
            p1s["prev"] = B.at("vs")

        ab_free = {0: B.at("zw"), 1: B.at("zw")}
        wr_done = {0: B.at("zw"), 1: B.at("zw")}
        ld_free = None

        def p2_chunk(widx, ci, r0, nr, c0, ncol):
            par = ci % 2
            slot = ci % 4
            buf = wch[slot]
            tq = tq_bf[par]
            B.dma("sync", f"w{slot}", buf[:],
                  wsrc[widx][r0:r0 + nr, c0:c0 + ncol],
                  waits=[ld_free[slot]])
            ld = B.at(f"w{slot}")
            B.comp("vector", "vs", lambda e, b=buf, sw=swrep[widx]:
                   e.tensor_scalar(out=b[:], in0=b[:], scalar1=sw[:],
                                   scalar2=-1.0, op0=ALU.mult, op1=ALU.max),
                   waits=[ld], sw=False)
            B.comp("vector", "vs", lambda e, b=buf, p=par: e.tensor_scalar(
                out=fq[p][:], in0=b[:], scalar1=1.0, scalar2=MAGIC,
                op0=ALU.min, op1=ALU.add),
                waits=[ab_free[par]], sw=False)
            ld_free[slot] = B.at("vs")
            B.comp("scalar", "ss", lambda e, t=tq, p=par: e.activation(
                out=t[:], in_=fq[p][:], func=AF.Identity, bias=nmagic_col[:],
                scale=1.0), waits=[wr_done[par], B.at("vs")], sw=False)
            ab_free[par] = B.at("ss")
            B.dma("scalar", f"st{par}",
                  wimg[widx].ap()[r0:r0 + nr, c0:c0 + ncol],
                  tq[:], waits=[B.at("ss")])
            wr_done[par] = B.at(f"st{par}")

        chlist = [wchunks(w) for w in range(3)]
        for w in range(3):
            for ci, ch in enumerate(chlist[w]):
                p1_chunk(w, ci, *ch)
            p1_stats(w)
        ld_free = {0: B.at("vs"), 1: B.at("vs"), 2: B.at("ss"), 3: B.at("ss")}
        for ci in range(64):
            p2_chunk(0, ci, *chlist[0][ci])
        gate_img_done = [B.at("st0"), B.at("st1")]
        for c in range(IC):
            B.dma("sync", "swg", wqTg[:, INNER * c:INNER * (c + 1)],
                  wg_img.ap()[0:INNER, 128 * c:128 * (c + 1)],
                  waits=gate_img_done, transpose=True)
        wqtg_done = B.at("swg")
        for ci in range(64):
            p2_chunk(1, ci, *chlist[1][ci])
        for ci in range(64):
            p2_chunk(2, ci, *chlist[2][ci])
        all_img_done = [B.at("st0"), B.at("st1")]

        if debug:
            B.dma("sync", "dbg", st_dbg.ap()[:, :], wstat[:], waits=[B.at("vs")])
            for i in range(3):
                B.dma("sync", "dbg", sw_dbg.ap()[i:i + 1, :], swrep[i][:],
                      waits=[B.at("vs")])
                B.dma("sync", "dbg", sw_dbg.ap()[3 + i:4 + i, :], sirep[i][:],
                      waits=[B.at("vs")])

        cast_done = B.at("qc")
        xaload = {}
        xbload = {}
        mix_done = {}
        res_done = {}
        hq_wr = {}

        def load_xtiles(k):
            if k >= NT:
                return
            w = []
            if k - 2 >= 0:
                w.append(("ts", mix_done.get(k - 2, 0)))
                w.append(("vs", res_done.get(k - 2, 0)))
            B.dma("sync", f"xa{k % 2}", xf32[k % 2][:],
                  xs[128 * k:128 * (k + 1), :], waits=w)
            xaload[k] = B.at(f"xa{k % 2}")
            B.dma("sync", f"xb{k % 2}", xbf[k % 2][:],
                  xbf_img.ap()[128 * k:128 * (k + 1), :],
                  waits=w + [cast_done])
            xbload[k] = B.at(f"xb{k % 2}")

        ystore = {}
        prev_fm_user = [0]
        wu_busy = {0: 0, 1: 0}
        wd_busy = {0: 0, 1: 0}
        smw_col = {j: B.at(f"sm{j}") for j in range(4)}
        ev_log_prev = [0]
        S = {}

        def emit_xfm(sb):
            t0 = 512 * sb
            fm_w = [cast_done, ("ts", prev_fm_user[0])]
            for c in range(IC):
                B.dma("sync", "sfm", fmbuf[:, 512 * c:512 * (c + 1)],
                      xbf_img.ap()[t0:t0 + 512, 128 * c:128 * (c + 1)],
                      waits=fm_w, transpose=True)
            S["xfm_done"] = B.at("sfm")

        def emit_a1(sb):
            t0 = 512 * sb
            for kk in range(BPS):
                k = BPS * sb + kk
                for c in range(IC):
                    B.comp("tensor", "ts", lambda e, c=c, kk=kk: e.matmul(
                        PB[4][:, 0:K],
                        fmbuf[:, 512 * c + 128 * kk:512 * c + 128 * (kk + 1)],
                        Wt_full[:, 128 * c:128 * c + K],
                        start=(c == 0), stop=(c == IC - 1)),
                        waits=([S["xfm_done"], wqtg_done, wt_done, p0_qx,
                                ("vs", max(ev_log_prev[0],
                                           S.get("pb4_free", 0)))]
                               if c == 0 else []),
                        inc=(1 if c == IC - 1 else 0))
                t_log = B.at("ts")
                B.comp("vector", "vs", lambda e: e.tensor_add(
                    out=logit_s[:], in0=PB[4][:, 0:K], in1=b_bc[:]), waits=[t_log])
                ev_log_prev[0] = B.cnt["vs"]
                B.comp("vector", "vs", lambda e: e.reduce_max(
                    out=col["amx"][:], in_=logit_s[:], axis=X))
                B.comp("vector", "vs", lambda e: e.tensor_scalar(
                    out=col["negmax"][:], in0=col["amx"][:], scalar1=-1.0,
                    scalar2=None, op0=ALU.mult))
                B.comp("scalar", "ss", lambda e: e.activation(
                    out=expv[:], in_=logit_s[:], func=AF.Exp,
                    bias=col["negmax"][:], scale=1.0, accum_out=col["sumexp"][:]),
                    waits=[B.at("vs")])
                B.comp("vector", "vs", lambda e: e.reciprocal(
                    out=col["rsx"][:], in_=col["sumexp"][:]), waits=[B.at("ss")])
                B.comp("vector", "vs", lambda e, kk=kk: e.tensor_scalar(
                    out=w_sm[:, K * kk:K * (kk + 1)], in0=expv[:],
                    scalar1=col["rsx"][:], scalar2=None,
                    op0=ALU.mult), waits=[smw_col[k % 4]])
                v_sm = B.at("vs")
                img = mimg[k % 4]
                B.dma("gpsimd", f"sm{k % 4}", AP(img, 0, [[MW + 1, 128], [1, K]]),
                      w_sm[:, K * kk:K * (kk + 1)], waits=[v_sm, mread[k - 4]])
                smw_col[k % 4] = B.at(f"sm{k % 4}")

        def emit_a2(sb):
            t0 = 512 * sb
            for kk in range(BPS):
                k = BPS * sb + kk
                B.dma("sync", f"sr{k % 4}", m_tm[:],
                      AP(mimg[k % 4], 0, [[MW, 128], [1, MW]]),
                      waits=[smw_col[k % 4], ("vs", B.cnt["vs"])])
                mread[k] = B.at(f"sr{k % 4}")
                B.comp("vector", "vs", lambda e: e.tensor_copy(
                    out=m_tf[:], in_=m_tm[:]), waits=[mread[k]])
                v_mtf = B.at("vs")
                B.comp("tensor", "ts", lambda e: e.transpose(
                    PB[5][:, 0:128], m_tf[:, 0:128], ident_f[:]),
                    waits=[v_mtf, g_ident])
                B.comp("tensor", "ts", lambda e: e.transpose(
                    PB[5][0:32, 128:256], m_tf[:, 128:MW], ident_f[:]))
                t_mt = B.at("ts")
                B.comp("vector", "vs", lambda e: e.tensor_copy(
                    out=m1t[:], in_=PB[5][:, 0:128]), waits=[t_mt])
                B.comp("vector", "vs", lambda e: e.tensor_copy(
                    out=m2t[:], in_=PB[5][0:32, 128:256]))
                v_mt = B.at("vs")
                for hf in range(2):
                    B.comp("tensor", "ts", lambda e, hf=hf, k=k: e.matmul(
                        PB[hf][:, :], m1t[:], xbf[k % 2][:, 512 * hf:512 * (hf + 1)],
                        start=True, stop=False),
                        waits=([v_mt, xbload[k], xbload[k + 1],
                                ("vs", res_done.get(k - 1, 0))] if hf == 0 else []),
                        inc=0)
                    B.comp("tensor", "ts", lambda e, hf=hf, k=k: e.matmul(
                        PB[hf][:, :], m2t[:],
                        xbf[(k + 1) % 2][0:32, 512 * hf:512 * (hf + 1)],
                        start=False, stop=True), inc=1)
                mix_done[k] = B.cnt["ts"]
                t_mix = B.at("ts")
                for hf in range(2):
                    B.comp("vector", "vs", lambda e, hf=hf, k=k: e.tensor_add(
                        out=r_t[:, 512 * hf:512 * (hf + 1)],
                        in0=xf32[k % 2][:, 512 * hf:512 * (hf + 1)],
                        in1=PB[hf][:, :]),
                        waits=([t_mix, xaload[k]] if hf == 0 else []))
                res_done[k] = B.cnt["vs"]
                load_xtiles(k + 2)
                B.comp("vector", "vs", lambda e: e.reduce_sum(
                    out=col["sum"][:], in_=r_t[:], axis=X))
                B.comp("vector", "vs", lambda e: e.tensor_scalar(
                    out=col["mean"][:], in0=col["sum"][:], scalar1=float(2.0 ** -10),
                    scalar2=None, op0=ALU.mult))
                B.comp("scalar", "ss", lambda e: e.activation(
                    out=u_scr[:], in_=r_t[:], func=AF.Square, bias=0.0, scale=1.0,
                    accum_out=col["sq"][:]), waits=[B.at("vs")])
                B.comp("vector", "vs", lambda e: e.tensor_scalar(
                    out=col["msq"][:], in0=col["sq"][:], scalar1=float(2.0 ** -10),
                    scalar2=None, op0=ALU.mult), waits=[B.at("ss")])
                B.comp("vector", "vs", lambda e: e.tensor_mul(
                    out=col["m2c"][:], in0=col["mean"][:], in1=col["mean"][:]))
                B.comp("vector", "vs", lambda e: e.tensor_sub(
                    out=col["var"][:], in0=col["msq"][:], in1=col["m2c"][:]))
                B.comp("scalar", "ss", lambda e: e.activation(
                    out=col["stdv"][:], in_=col["var"][:], func=AF.Sqrt,
                    bias=eps_col[:], scale=1.0), waits=[B.at("vs")])
                B.comp("vector", "vs", lambda e: e.reciprocal(
                    out=col["rstd"][:], in_=col["stdv"][:]), waits=[B.at("ss")])
                B.comp("vector", "vs", lambda e, kk=kk: e.tensor_scalar(
                    out=h_sb[:, D * kk:D * (kk + 1)], in0=r_t[:],
                    scalar1=col["mean"][:], scalar2=col["rstd"][:],
                    op0=ALU.subtract, op1=ALU.mult))
                v_h = B.at("vs")
                if debug:
                    B.dma("sync", "dbg", h_dbg.ap()[128 * k:128 * (k + 1), :],
                          h_sb[:, D * kk:D * (kk + 1)], waits=[v_h])
                    B.comp("vector", "vs", lambda e, kk=kk: e.tensor_copy(
                        out=u_scr[:, 0:K], in_=w_sm[:, K * kk:K * (kk + 1)]))
                    B.dma("sync", "dbg", sm_dbg.ap()[128 * k:128 * (k + 1), :],
                          u_scr[:, 0:K], waits=[B.at("vs")])
                B.comp("vector", "vs", lambda e, kk=kk: e.reduce_max(
                    out=col["amx"][:], in_=h_sb[:, D * kk:D * (kk + 1)], axis=X,
                    apply_absolute_value=True))
                B.comp("vector", "vs", lambda e: e.tensor_scalar(
                    out=col["t0"][:], in0=col["amx"][:], scalar1=1e-5, scalar2=None,
                    op0=ALU.max))
                B.comp("vector", "vs", lambda e: e.reciprocal(
                    out=col["t1"][:], in_=col["t0"][:]))
                B.comp("vector", "vs", lambda e: e.tensor_scalar(
                    out=col["s_col"][:], in0=col["t1"][:], scalar1=127.0,
                    scalar2=None, op0=ALU.mult))
                B.comp("vector", "vs", lambda e, kk=kk: e.reciprocal(
                    out=sinv_sb[:, kk:kk + 1], in_=col["s_col"][:]))
                B.comp("vector", "vs", lambda e, kk=kk: e.tensor_scalar(
                    out=u_scr[:], in0=h_sb[:, D * kk:D * (kk + 1)],
                    scalar1=col["s_col"][:], scalar2=MAGIC,
                    op0=ALU.mult, op1=ALU.add))
                B.comp("vector", "vs", lambda e: e.tensor_scalar(
                    out=hq_t[:], in0=u_scr[:], scalar1=MAGIC, scalar2=None,
                    op0=ALU.subtract),
                    waits=([hq_wr[k - 1]] if k >= 1 else []))
                v_hq = B.at("vs")
                B.dma("sync", "shq", hq_img.ap()[128 * k:128 * (k + 1), :],
                      hq_t[:], waits=[v_hq])
                hq_wr[k] = B.at("shq")
                B.comp("vector", "vs", lambda e, kk=kk, w=sirep[0]: e.tensor_scalar(
                    out=cg_sb[:, kk:kk + 1], in0=sinv_sb[:, kk:kk + 1],
                    scalar1=w[:], scalar2=None, op0=ALU.mult))
                B.comp("vector", "vs", lambda e, kk=kk, w=sirep[1]: e.tensor_scalar(
                    out=cu_sb[:, kk:kk + 1], in0=sinv_sb[:, kk:kk + 1],
                    scalar1=w[:], scalar2=None, op0=ALU.mult))
                B.comp("vector", "vs", lambda e, kk=kk: e.tensor_scalar(
                    out=bb_scr[:], in0=ones_f[:], scalar1=cg_sb[:, kk:kk + 1],
                    scalar2=None, op0=ALU.mult))
                B.comp("tensor", "ts", lambda e: e.transpose(
                    PB[6][:, 0:128], bb_scr[:], ident_f[:]),
                    waits=[B.at("vs"), ("vs", B.cnt["vs"])])
                B.comp("vector", "vs", lambda e, kk=kk: e.tensor_copy(
                    out=cgb[:, 128 * kk:128 * (kk + 1)], in_=PB[6][:, 0:128]),
                    waits=[B.at("ts")])

        def emit_b1(sb):
            t0 = 512 * sb
            a_done = B.at("vs")
            for c in range(IC):
                B.dma("sync", "sfm", fmbuf[:, 512 * c:512 * (c + 1)],
                      hq_img.ap()[t0:t0 + 512, 128 * c:128 * (c + 1)],
                      waits=[hq_wr[BPS * sb + BPS - 1],
                             ("ts", B.cnt["ts"])], transpose=True)
            hqt_done = B.at("sfm")
            prev_fm_user[0] = 0

            B.comp("vector", "vs", lambda e: e.memset(maxac[:], 0.0))
            B.comp("vector", "vs", lambda e: e.memset(minac[:], 0.0))
            v_am0 = B.at("vs")

            g_evac = {0: a_done, 1: a_done}
            u_evac = {0: B.at("vs"), 1: B.at("vs")}
            for ob in range(OB):
                wub = wuT[ob % 2]
                ww = all_img_done + [("ts", wu_busy[ob % 2])]
                for c in range(IC):
                    B.dma("sync", f"su{ob % 2}", wub[:, 512 * c:512 * (c + 1)],
                          wu_img.ap()[512 * ob:512 * (ob + 1),
                                      128 * c:128 * (c + 1)],
                          waits=ww if c == 0 else [], transpose=True)
                wu_ld = B.at(f"su{ob % 2}")
                for oc in range(4):
                    g = 4 * ob + oc
                    pg, pu = PB[g % 2], PB[2 + g % 2]
                    for c in range(IC):
                        B.comp("tensor", "ts", lambda e, c=c, g=g, pg=pg: e.matmul(
                            pg[:, :],
                            wqTg[:, INNER * c + 128 * g:INNER * c + 128 * (g + 1)],
                            fmbuf[:, 512 * c:512 * (c + 1)],
                            start=(c == 0), stop=(c == IC - 1)),
                            waits=([hqt_done, wqtg_done, g_evac[g % 2]]
                                   if c == 0 else []),
                            inc=(1 if c == IC - 1 else 0))
                    t_g = B.at("ts")
                    for c in range(IC):
                        B.comp("tensor", "ts", lambda e, c=c, oc=oc, pu=pu, wub=wub:
                               e.matmul(
                            pu[:, :],
                            wub[:, 512 * c + 128 * oc:512 * c + 128 * (oc + 1)],
                            fmbuf[:, 512 * c:512 * (c + 1)],
                            start=(c == 0), stop=(c == IC - 1)),
                            waits=([wu_ld, u_evac[g % 2]] if c == 0 else []),
                            inc=(1 if c == IC - 1 else 0))
                    t_u = B.at("ts")
                    wu_busy[ob % 2] = B.cnt["ts"]
                    prev_fm_user[0] = B.cnt["ts"]
                    B.comp("vector", "vs", lambda e, pg=pg: e.tensor_mul(
                        out=gated[:], in0=pg[:, :], in1=cgb[:]), waits=[t_g],
                        sw=True)
                    g_evac[g % 2] = B.at("vs")
                    B.comp("scalar", "ss", lambda e: e.activation(
                        out=silu_g[:], in_=gated[:], func=AF.Silu, bias=0.0,
                        scale=1.0), waits=[B.at("vs")], sw=False)
                    B.comp("vector", "vs", lambda e, pu=pu, g=g: e.tensor_mul(
                        out=P_sb[:, 512 * g:512 * (g + 1)], in0=pu[:, :],
                        in1=silu_g[:]), waits=[t_u, B.at("ss")], sw=True)
                    u_evac[g % 2] = B.at("vs")
                    B.comp("vector", "vs", lambda e, g=g: e.tensor_tensor(
                        out=maxac[:], in0=maxac[:],
                        in1=P_sb[:, 512 * g:512 * (g + 1)], op=ALU.max),
                        waits=[v_am0])
                    B.comp("vector", "vs", lambda e, g=g: e.tensor_tensor(
                        out=minac[:], in0=minac[:],
                        in1=P_sb[:, 512 * g:512 * (g + 1)], op=ALU.min))
            if debug:
                B.comp("vector", "vs", lambda e, sb=sb: e.tensor_copy(
                    out=gated[:], in_=P_sb[:, 0:512]))
                B.dma("sync", "dbg", P_dbg.ap()[sb, :, 0:512], gated[:],
                      waits=[B.at("vs")])
            B.comp("vector", "vs", lambda e: e.tensor_scalar(
                out=amax_acc[:], in0=minac[:], scalar1=-1.0, scalar2=None,
                op0=ALU.mult))
            B.comp("vector", "vs", lambda e: e.tensor_tensor(
                out=amax_acc[:], in0=amax_acc[:], in1=maxac[:], op=ALU.max))
            S["g_amax"] = B.at("vs")

        def emit_stats(sb):
            t0 = 512 * sb
            for tb in range(BPS):
                B.comp("tensor", "ts", lambda e, tb=tb: e.transpose(
                    PB[6 + tb % 2][:, 0:128], amax_acc[:, 128 * tb:128 * (tb + 1)],
                    ident_f[:]), waits=[S["g_amax"], ("vs", B.cnt["vs"])])
                B.comp("vector", "vs", lambda e, tb=tb: e.reduce_max(
                    out=amp_sb[:, tb:tb + 1], in_=PB[6 + tb % 2][:, 0:128], axis=X),
                    waits=[B.at("ts")])
                B.comp("vector", "vs", lambda e, tb=tb: e.tensor_scalar(
                    out=col["t0"][:], in0=amp_sb[:, tb:tb + 1],
                    scalar1=cu_sb[:, tb:tb + 1], scalar2=1e-5,
                    op0=ALU.mult, op1=ALU.max))
                B.comp("vector", "vs", lambda e: e.reciprocal(
                    out=col["t1"][:], in_=col["t0"][:]))
                B.comp("vector", "vs", lambda e, tb=tb: e.tensor_scalar(
                    out=s2_sb[:, tb:tb + 1], in0=col["t1"][:], scalar1=127.0,
                    scalar2=None, op0=ALU.mult))
                B.comp("vector", "vs", lambda e, tb=tb: e.reciprocal(
                    out=s2i_sb[:, tb:tb + 1], in_=s2_sb[:, tb:tb + 1]))
                B.comp("vector", "vs", lambda e, tb=tb: e.tensor_mul(
                    out=qm_sb[:, tb:tb + 1], in0=cu_sb[:, tb:tb + 1],
                    in1=s2_sb[:, tb:tb + 1]))
                B.comp("vector", "vs", lambda e, tb=tb, w=sirep[2]: e.tensor_scalar(
                    out=cd_sb[:, tb:tb + 1], in0=s2i_sb[:, tb:tb + 1],
                    scalar1=w[:], scalar2=None, op0=ALU.mult))
                B.comp("vector", "vs", lambda e, tb=tb: e.tensor_scalar(
                    out=bb_scr[:], in0=ones_f[:], scalar1=qm_sb[:, tb:tb + 1],
                    scalar2=None, op0=ALU.mult))
                B.comp("tensor", "ts", lambda e, tb=tb: e.transpose(
                    PB[6 + tb % 2][:, 0:128], bb_scr[:], ident_f[:]),
                    waits=[("vs", B.cnt["vs"])])
                B.comp("vector", "vs", lambda e, tb=tb: e.tensor_copy(
                    out=qmb[:, 128 * tb:128 * (tb + 1)],
                    in_=PB[6 + tb % 2][:, 0:128]), waits=[B.at("ts")])
            S["v_qmb"] = B.at("vs")

        def emit_down(sb):
            t0 = 512 * sb
            v_psum_free = ("vs", B.cnt["vs"])
            uq = [gated, u1_s]
            iq_ct = 0
            for g in range(OC):
                wb = wdT[g % 2]
                B.dma("scalar", f"sd{g % 2}", wb[:],
                      wd_img.ap()[0:D, 128 * g:128 * (g + 1)],
                      waits=all_img_done + [("ts", wd_busy[g % 2])],
                      transpose=True)
                wd_ld = B.at(f"sd{g % 2}")
                B.comp("gpsimd", "gs", lambda e, g=g, u=uq[g % 2]: e.tensor_mul(
                    out=u[:], in0=P_sb[:, 512 * g:512 * (g + 1)], in1=qmb[:]),
                    waits=[S["v_qmb"]], sw=False)
                B.comp("vector", "vs", lambda e, g=g, u=uq[g % 2]: e.tensor_scalar(
                    out=P_sb[:, 512 * g:512 * (g + 1)], in0=u[:], scalar1=MAGIC,
                    scalar2=MAGIC, op0=ALU.add, op1=ALU.subtract),
                    waits=[B.at("gs")], sw=True)
                q3_at = B.at("vs")
                if debug and g == 0:
                    B.comp("vector", "vs", lambda e, sb=sb: e.tensor_copy(
                        out=bb_scr[:], in_=P_sb[:, 0:128]), waits=[q3_at])
                    B.dma("sync", "dbg", iq_dbg.ap()[sb, :, 0:128], bb_scr[:],
                          waits=[B.at("vs")])
                for tb in range(BPS):
                    for jh in range(2):
                        B.comp("tensor", "ts", lambda e, g=g, tb=tb, jh=jh, wb=wb:
                               e.matmul(
                            PB[2 * tb + jh][:, :],
                            P_sb[:, 512 * g + 128 * tb:512 * g + 128 * (tb + 1)],
                            wb[:, 512 * jh:512 * (jh + 1)],
                            start=(g == 0), stop=(g == OC - 1)),
                            waits=([wd_ld, q3_at, v_psum_free]
                                   if (tb == 0 and jh == 0) else []),
                            inc=(1 if (tb == BPS - 1 and jh == 1) else 0))
                wd_busy[g % 2] = B.cnt["ts"]
            S["t_down"] = B.at("ts")

        def emit_evac(sb):
            t0 = 512 * sb
            for tb in range(BPS):
                for jh in range(2):
                    B.comp("vector", "vs", lambda e, tb=tb, jh=jh:
                           e.scalar_tensor_tensor(
                        out=r_t[:, 512 * jh:512 * (jh + 1)],
                        in0=PB[2 * tb + jh][:, :], scalar=cd_sb[:, tb:tb + 1],
                        in1=h_sb[:, D * tb + 512 * jh:D * tb + 512 * (jh + 1)],
                        op0=ALU.mult, op1=ALU.add),
                        waits=([S["t_down"]] if (tb == 0 and jh == 0) else []))
                    if tb == 2 and jh == 0:
                        S["pb4_free"] = B.cnt["vs"]
                B.comp("vector", "vs", lambda e: e.reduce_sum(
                    out=col["sum"][:], in_=r_t[:], axis=X))
                B.comp("vector", "vs", lambda e: e.tensor_scalar(
                    out=col["mean"][:], in0=col["sum"][:], scalar1=float(2.0 ** -10),
                    scalar2=None, op0=ALU.mult))
                B.comp("scalar", "ss", lambda e: e.activation(
                    out=u_scr[:], in_=r_t[:], func=AF.Square, bias=0.0, scale=1.0,
                    accum_out=col["sq"][:]), waits=[B.at("vs")])
                B.comp("vector", "vs", lambda e: e.tensor_scalar(
                    out=col["msq"][:], in0=col["sq"][:], scalar1=float(2.0 ** -10),
                    scalar2=None, op0=ALU.mult), waits=[B.at("ss")])
                B.comp("vector", "vs", lambda e: e.tensor_mul(
                    out=col["m2c"][:], in0=col["mean"][:], in1=col["mean"][:]))
                B.comp("vector", "vs", lambda e: e.tensor_sub(
                    out=col["var"][:], in0=col["msq"][:], in1=col["m2c"][:]))
                B.comp("scalar", "ss", lambda e: e.activation(
                    out=col["stdv"][:], in_=col["var"][:], func=AF.Sqrt,
                    bias=eps_col[:], scale=1.0), waits=[B.at("vs")])
                B.comp("vector", "vs", lambda e: e.reciprocal(
                    out=col["rstd"][:], in_=col["stdv"][:]), waits=[B.at("ss")])
                ysl = (BPS * sb + tb) % 2
                yt = ytile[ysl]
                B.comp("vector", "vs", lambda e, yt=yt: e.tensor_scalar(
                    out=yt[:], in0=r_t[:], scalar1=col["mean"][:],
                    scalar2=col["rstd"][:], op0=ALU.subtract, op1=ALU.mult),
                    waits=([ystore[ysl]] if ysl in ystore else []))
                B.dma("sync", f"sy{ysl}", yo[t0 + 128 * tb:t0 + 128 * (tb + 1), :],
                      yt[:], waits=[B.at("vs")])
                ystore[ysl] = B.at(f"sy{ysl}")


        load_xtiles(0)
        load_xtiles(1)
        for sb in range(NSB):
            emit_xfm(sb)
            emit_a1(sb)
            emit_a2(sb)
            emit_b1(sb)
            emit_stats(sb)
            emit_down(sb)
            emit_evac(sb)
        B.ev("sync", None, waits=[B.at("sy0"), B.at("sy1")])

        with nc.Block() as block:
            def run_engine(ename, engine):
                for eng, fn, waits, incs in B.events:
                    if eng != ename:
                        continue
                    for sname, val in waits:
                        if val > 0:
                            engine.wait_ge(sems[sname], val)
                    if fn is None:
                        continue
                    inst = fn(engine)
                    for sname, n in incs:
                        inst = inst.then_inc(sems[sname], n)

            block.sync(lambda e: run_engine("sync", e))
            block.vector(lambda e: run_engine("vector", e))
            block.scalar(lambda e: run_engine("scalar", e))
            block.tensor(lambda e: run_engine("tensor", e))
            block.gpsimd(lambda e: run_engine("gpsimd", e))

    return nc


_NC_CACHE = [None]


def kernel(L0_attn_x, L0_attn_W, L0_attn_b, L0_ffn_w_gate, L0_ffn_w_up,
           L0_ffn_w_down):
    x = np.ascontiguousarray(np.asarray(L0_attn_x, dtype=np.float32))
    Wm = np.ascontiguousarray(np.asarray(L0_attn_W, dtype=np.float32))
    bv = np.asarray(L0_attn_b, dtype=np.float32)
    wg = np.ascontiguousarray(np.asarray(L0_ffn_w_gate, dtype=np.float32))
    wu = np.ascontiguousarray(np.asarray(L0_ffn_w_up, dtype=np.float32))
    wd = np.ascontiguousarray(np.asarray(L0_ffn_w_down, dtype=np.float32))
    Bn, Ln, Dn = x.shape
    assert (Bn, Ln, Dn) == (2, 8192, 1024)

    bb = np.ascontiguousarray(np.broadcast_to(bv[None, :], (128, K)))
    in_maps = []
    for core in range(8):
        b, q = divmod(core, 4)
        t0 = q * T
        shard = np.zeros((TP, D), dtype=np.float32)
        end = min(Ln, t0 + TP)
        shard[0:end - t0] = x[b, t0:end]
        in_maps.append({"xs": shard, "Wm": Wm, "bb": bb,
                        "wg": wg, "wu": wu, "wd": wd})

    if _NC_CACHE[0] is None:
        _NC_CACHE[0] = build_core_program()
    nc = _NC_CACHE[0]
    res = run_bass_kernel_spmd(nc, in_maps, core_ids=list(range(8)))
    out = np.empty((2, 8192, 1024), dtype=np.float32)
    for core in range(8):
        b, q = divmod(core, 4)
        out[b, q * T:(q + 1) * T] = res.results[core]["yo"]
    return out


if __name__ == "__main__":
    pass


# version 33
# speedup vs baseline: 1.2006x; 1.0358x over previous
import contextlib
import numpy as np
import concourse.bass as bass
import concourse.mybir as mybir
from concourse.ap import AP
from concourse.bass_utils import run_bass_kernel_spmd

F32 = mybir.dt.float32
BF16 = mybir.dt.bfloat16
I32 = mybir.dt.int32
AF = mybir.ActivationFunctionType
ALU = mybir.AluOpType
X = mybir.AxisListType.X

MAGIC = 12582912.0
D = 1024
K = 32
INNER = 4096
T = 2048
TP = 2176
NT = TP // 128
NBLK = 16
NSB = 4
BPS = 4
IC = 8
OC = 32
OB = 8
MW = 160


class Builder:

    def __init__(self):
        self.events = []
        self.cnt = {}

    def sem_names(self):
        return sorted(self.cnt)

    def ev(self, eng, fn, waits=(), incs=()):
        w = {}
        for name, val in waits:
            w[name] = max(w.get(name, 0), val)
        for name, n in incs:
            self.cnt[name] = self.cnt.get(name, 0) + n
        self.events.append((eng, fn, sorted(w.items()), list(incs)))

    def dma(self, eng, sem, out, in_, waits=(), transpose=False):
        self.cnt.setdefault(sem, 0)
        if transpose:
            fn = lambda e, o=out, i=in_: e.dma_start_transpose(o, i)
        else:
            fn = lambda e, o=out, i=in_: e.dma_start(out=o, in_=i)
        self.ev(eng, fn, waits, incs=[(sem, 16)])
        return self.cnt[sem]

    def comp(self, eng, sem, fn, waits=(), inc=1, sw=True):
        self.cnt.setdefault(sem, 0)
        waits = list(waits)
        if sw and eng in ("vector", "scalar", "gpsimd"):
            waits.append((sem, self.cnt[sem]))
        self.ev(eng, fn, waits, incs=([(sem, inc)] if inc else []))
        return self.cnt[sem]

    def at(self, sem):
        return (sem, self.cnt.get(sem, 0))


def build_core_program(debug=False):
    nc = bass.Bass("TRN2", target_bir_lowering=False)
    ikind = "ExternalOutput" if debug else "Internal"

    xs = nc.dram_tensor("xs", [TP, D], F32, kind="ExternalInput").ap()
    Wm = nc.dram_tensor("Wm", [K, D], F32, kind="ExternalInput").ap()
    bb = nc.dram_tensor("bb", [128, K], F32, kind="ExternalInput").ap()
    wg = nc.dram_tensor("wg", [INNER, D], F32, kind="ExternalInput").ap()
    wu = nc.dram_tensor("wu", [INNER, D], F32, kind="ExternalInput").ap()
    wd = nc.dram_tensor("wd", [D, INNER], F32, kind="ExternalInput").ap()
    yo = nc.dram_tensor("yo", [T, D], F32, kind="ExternalOutput").ap()

    xbf_img = nc.dram_tensor("xbf_img", [TP, D], BF16, kind=ikind)
    hq_img = nc.dram_tensor("hq_img", [T, D], BF16, kind=ikind)
    wg_img = nc.dram_tensor("wg_img", [INNER, D], BF16, kind=ikind)
    wu_img = nc.dram_tensor("wu_img", [INNER, D], BF16, kind=ikind)
    wd_img = nc.dram_tensor("wd_img", [D, INNER], BF16, kind=ikind)
    wm_img = nc.dram_tensor("wm_img", [128, D], BF16, kind="Internal")
    mimg = [nc.dram_tensor(f"mimg{j}", [(MW + 1) * 128], BF16, kind="Internal")
            for j in range(4)]
    h_dbg = nc.dram_tensor("h_dbg", [T, D], F32, kind=ikind) if debug else None
    sm_dbg = nc.dram_tensor("sm_dbg", [T, K], F32, kind=ikind) if debug else None
    P_dbg = nc.dram_tensor("P_dbg", [NSB, 128, OC * 512], F32, kind=ikind) if debug else None
    st_dbg = nc.dram_tensor("st_dbg", [1, 32], F32, kind=ikind) if debug else None
    sw_dbg = nc.dram_tensor("sw_dbg", [6, 128], F32, kind=ikind) if debug else None
    rc_dbg = nc.dram_tensor("rc_dbg", [3, 128, 64], F32, kind=ikind) if debug else None
    iq_dbg = nc.dram_tensor("iq_dbg", [NSB, 128, OC * 512], F32, kind=ikind) if debug else None

    B = Builder()
    es = contextlib.ExitStack()
    with es:
        def sb_(name, shape, dt):
            return es.enter_context(nc.sbuf_tensor(name, shape, dt))

        ones_f = sb_("ones_f", [128, 128], F32)
        ident_f = sb_("ident_f", [128, 128], F32)
        onescol = sb_("onescol", [128, 1], F32)
        b_bc = sb_("b_bc", [128, K], F32)
        eps_col = sb_("eps_col", [128, 1], F32)
        nhalf_col = sb_("nhalf_col", [128, 1], F32)
        phalf_col = sb_("phalf_col", [128, 1], F32)
        magic_col = sb_("magic_col", [128, 1], F32)
        nmagic_col = sb_("nmagic_col", [128, 1], F32)
        zeros_m = sb_("zeros_m", [128, MW + 1], BF16)
        Wt_full = sb_("Wt_full", [128, IC * 128], BF16)
        swrep = [sb_(f"swrep{i}", [128, 1], F32) for i in range(3)]
        sirep = [sb_(f"sirep{i}", [128, 1], F32) for i in range(3)]
        wstat = sb_("wstat", [1, 32], F32)

        wch = [sb_(f"wch{j}", [128, 512], F32) for j in range(4)]
        fq = [sb_(f"fq{j}", [128, 512], F32) for j in range(2)]
        wi32 = [sb_("wi32_0", [128, 512], I32)]
        tq_bf = [sb_(f"tq_bf{j}", [128, 512], BF16) for j in range(2)]
        rcols = sb_("rcols", [128, 64], F32)
        r8 = sb_("r8", [128, 8], F32)
        pc_col = sb_("pc_col", [128, 1], F32)
        ones8 = sb_("ones8", [128, 8], F32)
        W_f = [wch[0][0:K, :], wch[1][0:K, :]]
        W_bt = sb_("W_bt", [K, D], BF16)

        wqTg = sb_("wqTg", [128, IC * INNER], BF16)
        wuT = [sb_(f"wuT{j}", [128, IC * 512], BF16) for j in range(2)]
        wdT = [sb_(f"wdT{j}", [128, D], BF16) for j in range(2)]
        P_sb = sb_("P_sb", [128, OC * 512], BF16)
        h_sb = sb_("h_sb", [128, BPS * D], F32)
        fmbuf = sb_("fmbuf", [128, IC * 512], BF16)

        xf32 = [sb_(f"xf32_{j}", [128, D], F32) for j in range(2)]
        xbf = [sb_(f"xbf_{j}", [128, D], BF16) for j in range(2)]
        ytile = [sb_(f"ytile{j}", [128, D], F32) for j in range(2)]
        r_t = sb_("r_t", [128, D], F32)
        u_scr = sb_("u_scr", [128, D], F32)
        hq_t = sb_("hq_t", [128, D], BF16)
        gated = sb_("gated", [128, 512], F32)
        silu_g = sb_("silu_g", [128, 512], F32)
        amax_acc = sb_("amax_acc", [128, 512], F32)
        maxac = sb_("maxac", [128, 512], BF16)
        minac = sb_("minac", [128, 512], BF16)
        u1_s = silu_g
        cgb = sb_("cgb", [128, 512], F32)
        qmb = sb_("qmb", [128, 512], F32)
        bb_scr = sb_("bb_scr", [128, 128], F32)
        m_tm = sb_("m_tm", [128, MW], BF16)
        m_tf = sb_("m_tf", [128, MW], F32)
        m1t = sb_("m1t", [128, 128], BF16)
        m2t = sb_("m2t", [32, 128], BF16)
        w_sm = sb_("w_sm", [128, BPS * K], BF16)
        logit_s = sb_("logit_s", [128, K], F32)
        expv = sb_("expv", [128, K], F32)
        col = {name: sb_(f"c_{name}", [128, 1], F32) for name in
               ["sum", "mean", "sq", "msq", "m2c", "var", "stdv", "rstd",
                "amx", "t0", "t1", "s_col", "negmax", "sumexp", "rsx"]}
        sinv_sb = sb_("sinv_sb", [128, BPS], F32)
        cg_sb = sb_("cg_sb", [128, BPS], F32)
        cu_sb = sb_("cu_sb", [128, BPS], F32)
        cd_sb = sb_("cd_sb", [128, BPS], F32)
        amp_sb = sb_("amp_sb", [128, BPS], F32)
        s2_sb = sb_("s2_sb", [128, BPS], F32)
        s2i_sb = sb_("s2i_sb", [128, BPS], F32)
        qm_sb = sb_("qm_sb", [128, BPS], F32)

        PB = [es.enter_context(nc.psum_tensor(f"PB{j}", [128, 512], F32))
              for j in range(8)]

        sems = {}
        semnames = (["qc", "zw", "swt", "swg", "sc0", "sfm", "shq", "dbg",
                     "ts", "vs", "ss", "gs"]
                    + [f"w{j}" for j in range(4)]
                    + ["st0", "st1", "sy0", "sy1", "xa0", "xa1", "xb0", "xb1",
                       "su0", "su1", "sd0", "sd1"]
                    + [f"sm{j}" for j in range(4)] + [f"sr{j}" for j in range(4)])
        for s in semnames:
            sems[s] = es.enter_context(nc.semaphore(s))
            B.cnt[s] = 0

        B.comp("vector", "vs", lambda e: e.memset(ones_f[:], 1.0))
        B.comp("vector", "vs", lambda e: e.memset(onescol[:], 1.0))
        B.comp("vector", "vs", lambda e: e.memset(eps_col[:], 1e-5))
        B.comp("vector", "vs", lambda e: e.memset(nhalf_col[:], -0.5))
        B.comp("vector", "vs", lambda e: e.memset(phalf_col[:], 0.5))
        B.comp("vector", "vs", lambda e: e.memset(magic_col[:], MAGIC))
        B.comp("vector", "vs", lambda e: e.memset(nmagic_col[:], -MAGIC))
        B.comp("vector", "vs", lambda e: e.memset(zeros_m[:], 0.0))
        B.comp("vector", "vs", lambda e: e.memset(tq_bf[0][:], 0.0))
        B.comp("vector", "vs", lambda e: e.memset(tq_bf[1][:], 0.0))
        v_consts = B.at("vs")
        B.comp("gpsimd", "gs", lambda e: e.affine_select(
            ident_f[:], ones_f[:], pattern=[[-1, 128]], compare_op=ALU.is_equal,
            fill=0.0, base=0, channel_multiplier=1), waits=[v_consts])
        B.comp("gpsimd", "gs", lambda e: e.affine_select(
            ones8[:], ones_f[:, 0:8], pattern=[[-16, 8]], compare_op=ALU.is_ge,
            fill=0.0, base=0, channel_multiplier=1), waits=[v_consts])
        B.comp("gpsimd", "gs", lambda e: e.affine_select(
            ones8[:], ones8[:], pattern=[[16, 8]], compare_op=ALU.is_ge,
            fill=0.0, base=15, channel_multiplier=-1))
        g_ident = B.at("gs")
        B.dma("gpsimd", "qc", xbf_img.ap(), xs[:, :])
        mread = {}
        for j in range(4):
            B.dma("gpsimd", f"sr{j}", AP(mimg[j], 0, [[1, (MW + 1) * 128]]),
                  zeros_m[:], waits=[v_consts])
            mread[j - 4] = B.at(f"sr{j}")
        B.dma("sync", "sc0", W_f[0], Wm[:, 0:512])
        B.dma("sync", "sc0", W_f[1], Wm[:, 512:1024])
        B.comp("vector", "vs", lambda e: e.tensor_copy(out=W_bt[:, 0:512],
               in_=W_f[0]), waits=[B.at("sc0")])
        B.comp("vector", "vs", lambda e: e.tensor_copy(out=W_bt[:, 512:1024],
               in_=W_f[1]))
        B.dma("sync", "zw", wm_img.ap()[0:128, 0:512], tq_bf[0][:], waits=[v_consts])
        B.dma("sync", "zw", wm_img.ap()[0:128, 512:1024], tq_bf[1][:], waits=[v_consts])
        B.dma("sync", "zw", wm_img.ap()[0:K, :], W_bt[:], waits=[B.at("vs"), B.at("zw")])
        w_img_done = B.at("zw")
        for c in range(IC):
            B.dma("sync", "swt", Wt_full[:, 128 * c:128 * (c + 1)],
                  wm_img.ap()[0:128, 128 * c:128 * (c + 1)],
                  waits=[w_img_done], transpose=True)
        wt_done = B.at("swt")
        B.dma("sync", "sc0", b_bc[:], bb[:, :])
        p0_qx = B.at("sc0")

        wsrc = [wg, wu, wd]
        wimg = [wg_img, wu_img, wd_img]

        def wchunks(widx):
            if widx < 2:
                return [(128 * c, 128, 512 * s, 512) for c in range(32)
                        for s in range(2)]
            return [(128 * r, 128, 512 * s, 512) for r in range(8) for s in range(8)]

        for widx in range(3):
            prev_stats = B.at("vs")
            done_red = [B.at("vs"), B.at("vs"), B.at("ss"), B.at("ss")]
            for ci, (r0, nr, c0, ncol) in enumerate(wchunks(widx)):
                buf = wch[ci % 4]
                B.dma("sync", f"w{ci % 4}", buf[:],
                      wsrc[widx][r0:r0 + nr, c0:c0 + ncol],
                      waits=[done_red[ci % 4]])
                ld = B.at(f"w{ci % 4}")
                if ci % 4 < 2:
                    B.comp("vector", "vs", lambda e, b=buf, ci=ci: e.reduce_sum(
                        out=rcols[:, ci:ci + 1], in_=b[:], axis=X,
                        apply_absolute_value=True), waits=[ld, prev_stats],
                        sw=False)
                    done_red[ci % 4] = B.at("vs")
                else:
                    B.comp("scalar", "ss", lambda e, b=buf, ci=ci: e.activation(
                        out=u_scr[:, 0:512], in_=b[:], func=AF.Abs, bias=0.0,
                        scale=1.0, accum_out=rcols[:, ci:ci + 1]),
                        waits=[ld, prev_stats])
                    done_red[ci % 4] = B.at("ss")
            B.comp("vector", "vs", lambda e: e.reduce_sum(
                out=r8[:].rearrange("p (a b) -> p a b", b=1),
                in_=rcols[:].rearrange("p (a b) -> p a b", b=8), axis=X),
                waits=[B.at("ss")])
            B.comp("vector", "vs", lambda e: e.reduce_sum(
                out=pc_col[:], in_=r8[:], axis=X))
            v_acc = B.at("vs")
            if debug:
                B.dma("sync", "dbg", rc_dbg.ap()[widx, :, :], rcols[:],
                      waits=[v_acc])
            B.comp("tensor", "ts", lambda e: e.matmul(
                PB[7][0:1, 0:8], pc_col[:], ones8[:], start=True, stop=True),
                waits=[v_acc, g_ident, B.at("vs")])
            B.comp("vector", "vs", lambda e: e.tensor_copy(
                out=wstat[0:1, 16:24], in_=PB[7][0:1, 0:8]), waits=[B.at("ts")])
            B.comp("vector", "vs", lambda e: e.reduce_sum(
                out=wstat[0:1, 15:16], in_=wstat[0:1, 16:24], axis=X))
            w8 = widx * 5
            B.comp("vector", "vs", lambda e, o=wstat[0:1, w8:w8 + 1]: e.tensor_scalar(
                out=o, in0=wstat[0:1, 15:16], scalar1=float(2.0 ** -22), scalar2=None,
                op0=ALU.mult))
            B.comp("vector", "vs", lambda e, i=wstat[0:1, w8:w8 + 1],
                   o=wstat[0:1, w8 + 1:w8 + 2]: e.tensor_scalar(
                out=o, in0=i, scalar1=1e-5, scalar2=None, op0=ALU.max))
            B.comp("vector", "vs", lambda e, i=wstat[0:1, w8 + 1:w8 + 2],
                   o=wstat[0:1, w8 + 2:w8 + 3]: e.reciprocal(out=o, in_=i))
            B.comp("vector", "vs", lambda e, i=wstat[0:1, w8 + 2:w8 + 3],
                   o=wstat[0:1, w8 + 3:w8 + 4]: e.reciprocal(out=o, in_=i))
            v_stats = B.at("vs")
            for which, dst in ((2, swrep[widx]), (3, sirep[widx])):
                B.comp("tensor", "ts", lambda e, wv=wstat[0:1, w8 + which:w8 + which + 1]:
                       e.matmul(PB[7][:, 0:1], ones_f[0:1, 0:128], wv,
                                start=True, stop=True),
                       waits=[v_stats, B.at("vs")])
                B.comp("vector", "vs", lambda e, d=dst: e.tensor_copy(
                    out=d[:], in_=PB[7][:, 0:1]), waits=[B.at("ts")])

        if debug:
            B.dma("sync", "dbg", st_dbg.ap()[:, :], wstat[:], waits=[B.at("vs")])
            for i in range(3):
                B.dma("sync", "dbg", sw_dbg.ap()[i:i + 1, :], swrep[i][:],
                      waits=[B.at("vs")])
                B.dma("sync", "dbg", sw_dbg.ap()[3 + i:4 + i, :], sirep[i][:],
                      waits=[B.at("vs")])
        ab_free = {0: B.at("zw"), 1: B.at("zw")}
        wr_done = {0: B.at("zw"), 1: B.at("zw")}
        wi_free = {0: B.at("zw"), 1: B.at("zw")}
        ld_free = {0: B.at("vs"), 1: B.at("vs"), 2: B.at("ss"), 3: B.at("ss")}
        for widx in range(3):
            for ci, (r0, nr, c0, ncol) in enumerate(wchunks(widx)):
                par = ci % 2
                buf = wch[ci % 4]
                tq = tq_bf[par]
                B.dma("sync", f"w{ci % 4}", buf[:],
                      wsrc[widx][r0:r0 + nr, c0:c0 + ncol],
                      waits=[ld_free[ci % 4]])
                ld = B.at(f"w{ci % 4}")
                B.comp("vector", "vs", lambda e, b=buf, sw=swrep[widx]:
                       e.tensor_scalar(out=b[:], in0=b[:], scalar1=sw[:],
                                       scalar2=-1.0, op0=ALU.mult, op1=ALU.max),
                       waits=[ld], sw=False)
                B.comp("vector", "vs", lambda e, b=buf, p=par: e.tensor_scalar(
                    out=fq[p][:], in0=b[:], scalar1=1.0, scalar2=None,
                    op0=ALU.min),
                    waits=[ab_free[par]], sw=False)
                ld_free[ci % 4] = B.at("vs")
                B.comp("vector", "vs", lambda e, p=par: e.tensor_copy(
                    out=wi32[0][:], in_=fq[p][:]),
                    waits=[wi_free[0]], sw=False)
                ab_free[par] = ("vs", B.cnt["vs"])
                B.comp("vector", "vs", lambda e, t=tq, p=par: e.tensor_copy(
                    out=t[:], in_=wi32[0][:]), waits=[wr_done[par]], sw=False)
                wi_free[0] = ("vs", B.cnt["vs"])
                B.dma("scalar", f"st{par}", wimg[widx].ap()[r0:r0 + nr, c0:c0 + ncol],
                      tq[:], waits=[B.at("vs")])
                wr_done[par] = B.at(f"st{par}")
            if widx == 0:
                gate_img_done = [B.at("st0"), B.at("st1")]
        all_img_done = [B.at("st0"), B.at("st1")]

        for c in range(IC):
            B.dma("sync", "swg", wqTg[:, INNER * c:INNER * (c + 1)],
                  wg_img.ap()[0:INNER, 128 * c:128 * (c + 1)],
                  waits=gate_img_done, transpose=True)
        wqtg_done = B.at("swg")

        cast_done = B.at("qc")
        xaload = {}
        xbload = {}
        mix_done = {}
        res_done = {}
        hq_wr = {}

        def load_xtiles(k):
            if k >= NT:
                return
            w = []
            if k - 2 >= 0:
                w.append(("ts", mix_done.get(k - 2, 0)))
                w.append(("vs", res_done.get(k - 2, 0)))
            B.dma("sync", f"xa{k % 2}", xf32[k % 2][:],
                  xs[128 * k:128 * (k + 1), :], waits=w)
            xaload[k] = B.at(f"xa{k % 2}")
            B.dma("sync", f"xb{k % 2}", xbf[k % 2][:],
                  xbf_img.ap()[128 * k:128 * (k + 1), :],
                  waits=w + [cast_done])
            xbload[k] = B.at(f"xb{k % 2}")

        ystore = {}
        prev_fm_user = [0]
        wu_busy = {0: 0, 1: 0}
        wd_busy = {0: 0, 1: 0}
        smw_col = {j: B.at(f"sm{j}") for j in range(4)}
        ev_log_prev = [0]

        for sb in range(NSB):
            t0 = 512 * sb
            fm_w = [cast_done, ("ts", prev_fm_user[0])]
            for c in range(IC):
                B.dma("sync", "sfm", fmbuf[:, 512 * c:512 * (c + 1)],
                      xbf_img.ap()[t0:t0 + 512, 128 * c:128 * (c + 1)],
                      waits=fm_w, transpose=True)
            xfm_done = B.at("sfm")

            if sb == 0:
                load_xtiles(0)
                load_xtiles(1)

            for kk in range(BPS):
                k = BPS * sb + kk
                for c in range(IC):
                    B.comp("tensor", "ts", lambda e, c=c, kk=kk: e.matmul(
                        PB[4][:, 0:K],
                        fmbuf[:, 512 * c + 128 * kk:512 * c + 128 * (kk + 1)],
                        Wt_full[:, 128 * c:128 * c + K],
                        start=(c == 0), stop=(c == IC - 1)),
                        waits=([xfm_done, wqtg_done, wt_done, p0_qx,
                                ("vs", ev_log_prev[0])] if c == 0 else []),
                        inc=(1 if c == IC - 1 else 0))
                t_log = B.at("ts")
                B.comp("vector", "vs", lambda e: e.tensor_add(
                    out=logit_s[:], in0=PB[4][:, 0:K], in1=b_bc[:]), waits=[t_log])
                ev_log_prev[0] = B.cnt["vs"]
                B.comp("vector", "vs", lambda e: e.reduce_max(
                    out=col["amx"][:], in_=logit_s[:], axis=X))
                B.comp("vector", "vs", lambda e: e.tensor_scalar(
                    out=col["negmax"][:], in0=col["amx"][:], scalar1=-1.0,
                    scalar2=None, op0=ALU.mult))
                B.comp("scalar", "ss", lambda e: e.activation(
                    out=expv[:], in_=logit_s[:], func=AF.Exp,
                    bias=col["negmax"][:], scale=1.0, accum_out=col["sumexp"][:]),
                    waits=[B.at("vs")])
                B.comp("vector", "vs", lambda e: e.reciprocal(
                    out=col["rsx"][:], in_=col["sumexp"][:]), waits=[B.at("ss")])
                B.comp("vector", "vs", lambda e, kk=kk: e.tensor_scalar(
                    out=w_sm[:, K * kk:K * (kk + 1)], in0=expv[:],
                    scalar1=col["rsx"][:], scalar2=None,
                    op0=ALU.mult), waits=[smw_col[k % 4]])
                v_sm = B.at("vs")
                img = mimg[k % 4]
                B.dma("gpsimd", f"sm{k % 4}", AP(img, 0, [[MW + 1, 128], [1, K]]),
                      w_sm[:, K * kk:K * (kk + 1)], waits=[v_sm, mread[k - 4]])
                smw_col[k % 4] = B.at(f"sm{k % 4}")

            for kk in range(BPS):
                k = BPS * sb + kk
                B.dma("sync", f"sr{k % 4}", m_tm[:],
                      AP(mimg[k % 4], 0, [[MW, 128], [1, MW]]),
                      waits=[smw_col[k % 4], ("vs", B.cnt["vs"])])
                mread[k] = B.at(f"sr{k % 4}")
                B.comp("vector", "vs", lambda e: e.tensor_copy(
                    out=m_tf[:], in_=m_tm[:]), waits=[mread[k]])
                v_mtf = B.at("vs")
                B.comp("tensor", "ts", lambda e: e.transpose(
                    PB[5][:, 0:128], m_tf[:, 0:128], ident_f[:]),
                    waits=[v_mtf, g_ident])
                B.comp("tensor", "ts", lambda e: e.transpose(
                    PB[5][0:32, 128:256], m_tf[:, 128:MW], ident_f[:]))
                t_mt = B.at("ts")
                B.comp("vector", "vs", lambda e: e.tensor_copy(
                    out=m1t[:], in_=PB[5][:, 0:128]), waits=[t_mt])
                B.comp("vector", "vs", lambda e: e.tensor_copy(
                    out=m2t[:], in_=PB[5][0:32, 128:256]))
                v_mt = B.at("vs")
                for hf in range(2):
                    B.comp("tensor", "ts", lambda e, hf=hf, k=k: e.matmul(
                        PB[hf][:, :], m1t[:], xbf[k % 2][:, 512 * hf:512 * (hf + 1)],
                        start=True, stop=False),
                        waits=([v_mt, xbload[k], xbload[k + 1],
                                ("vs", res_done.get(k - 1, 0))] if hf == 0 else []),
                        inc=0)
                    B.comp("tensor", "ts", lambda e, hf=hf, k=k: e.matmul(
                        PB[hf][:, :], m2t[:],
                        xbf[(k + 1) % 2][0:32, 512 * hf:512 * (hf + 1)],
                        start=False, stop=True), inc=1)
                mix_done[k] = B.cnt["ts"]
                t_mix = B.at("ts")
                for hf in range(2):
                    B.comp("vector", "vs", lambda e, hf=hf, k=k: e.tensor_add(
                        out=r_t[:, 512 * hf:512 * (hf + 1)],
                        in0=xf32[k % 2][:, 512 * hf:512 * (hf + 1)],
                        in1=PB[hf][:, :]),
                        waits=([t_mix, xaload[k]] if hf == 0 else []))
                res_done[k] = B.cnt["vs"]
                load_xtiles(k + 2)
                B.comp("vector", "vs", lambda e: e.reduce_sum(
                    out=col["sum"][:], in_=r_t[:], axis=X))
                B.comp("vector", "vs", lambda e: e.tensor_scalar(
                    out=col["mean"][:], in0=col["sum"][:], scalar1=float(2.0 ** -10),
                    scalar2=None, op0=ALU.mult))
                B.comp("scalar", "ss", lambda e: e.activation(
                    out=u_scr[:], in_=r_t[:], func=AF.Square, bias=0.0, scale=1.0,
                    accum_out=col["sq"][:]), waits=[B.at("vs")])
                B.comp("vector", "vs", lambda e: e.tensor_scalar(
                    out=col["msq"][:], in0=col["sq"][:], scalar1=float(2.0 ** -10),
                    scalar2=None, op0=ALU.mult), waits=[B.at("ss")])
                B.comp("vector", "vs", lambda e: e.tensor_mul(
                    out=col["m2c"][:], in0=col["mean"][:], in1=col["mean"][:]))
                B.comp("vector", "vs", lambda e: e.tensor_sub(
                    out=col["var"][:], in0=col["msq"][:], in1=col["m2c"][:]))
                B.comp("scalar", "ss", lambda e: e.activation(
                    out=col["stdv"][:], in_=col["var"][:], func=AF.Sqrt,
                    bias=eps_col[:], scale=1.0), waits=[B.at("vs")])
                B.comp("vector", "vs", lambda e: e.reciprocal(
                    out=col["rstd"][:], in_=col["stdv"][:]), waits=[B.at("ss")])
                B.comp("vector", "vs", lambda e, kk=kk: e.tensor_scalar(
                    out=h_sb[:, D * kk:D * (kk + 1)], in0=r_t[:],
                    scalar1=col["mean"][:], scalar2=col["rstd"][:],
                    op0=ALU.subtract, op1=ALU.mult))
                v_h = B.at("vs")
                if debug:
                    B.dma("sync", "dbg", h_dbg.ap()[128 * k:128 * (k + 1), :],
                          h_sb[:, D * kk:D * (kk + 1)], waits=[v_h])
                    B.comp("vector", "vs", lambda e, kk=kk: e.tensor_copy(
                        out=u_scr[:, 0:K], in_=w_sm[:, K * kk:K * (kk + 1)]))
                    B.dma("sync", "dbg", sm_dbg.ap()[128 * k:128 * (k + 1), :],
                          u_scr[:, 0:K], waits=[B.at("vs")])
                B.comp("vector", "vs", lambda e, kk=kk: e.reduce_max(
                    out=col["amx"][:], in_=h_sb[:, D * kk:D * (kk + 1)], axis=X,
                    apply_absolute_value=True))
                B.comp("vector", "vs", lambda e: e.tensor_scalar(
                    out=col["t0"][:], in0=col["amx"][:], scalar1=1e-5, scalar2=None,
                    op0=ALU.max))
                B.comp("vector", "vs", lambda e: e.reciprocal(
                    out=col["t1"][:], in_=col["t0"][:]))
                B.comp("vector", "vs", lambda e: e.tensor_scalar(
                    out=col["s_col"][:], in0=col["t1"][:], scalar1=127.0,
                    scalar2=None, op0=ALU.mult))
                B.comp("vector", "vs", lambda e, kk=kk: e.reciprocal(
                    out=sinv_sb[:, kk:kk + 1], in_=col["s_col"][:]))
                B.comp("vector", "vs", lambda e, kk=kk: e.tensor_scalar(
                    out=u_scr[:], in0=h_sb[:, D * kk:D * (kk + 1)],
                    scalar1=col["s_col"][:], scalar2=None, op0=ALU.mult))
                B.comp("vector", "vs", lambda e: e.tensor_copy(
                    out=wi32[0][:], in_=u_scr[:, 0:512]))
                B.comp("vector", "vs", lambda e: e.tensor_copy(
                    out=hq_t[:, 0:512], in_=wi32[0][:]),
                    waits=([hq_wr[k - 1]] if k >= 1 else []))
                B.comp("vector", "vs", lambda e: e.tensor_copy(
                    out=wi32[0][:], in_=u_scr[:, 512:D]))
                B.comp("vector", "vs", lambda e: e.tensor_copy(
                    out=hq_t[:, 512:D], in_=wi32[0][:]))
                v_hq = B.at("vs")
                B.dma("sync", "shq", hq_img.ap()[128 * k:128 * (k + 1), :], hq_t[:],
                      waits=[v_hq])
                hq_wr[k] = B.at("shq")
                B.comp("vector", "vs", lambda e, kk=kk, w=sirep[0]: e.tensor_scalar(
                    out=cg_sb[:, kk:kk + 1], in0=sinv_sb[:, kk:kk + 1],
                    scalar1=w[:], scalar2=None, op0=ALU.mult))
                B.comp("vector", "vs", lambda e, kk=kk, w=sirep[1]: e.tensor_scalar(
                    out=cu_sb[:, kk:kk + 1], in0=sinv_sb[:, kk:kk + 1],
                    scalar1=w[:], scalar2=None, op0=ALU.mult))
                B.comp("vector", "vs", lambda e, kk=kk: e.tensor_scalar(
                    out=bb_scr[:], in0=ones_f[:], scalar1=cg_sb[:, kk:kk + 1],
                    scalar2=None, op0=ALU.mult))
                B.comp("tensor", "ts", lambda e: e.transpose(
                    PB[6][:, 0:128], bb_scr[:], ident_f[:]),
                    waits=[B.at("vs"), ("vs", B.cnt["vs"])])
                B.comp("vector", "vs", lambda e, kk=kk: e.tensor_copy(
                    out=cgb[:, 128 * kk:128 * (kk + 1)], in_=PB[6][:, 0:128]),
                    waits=[B.at("ts")])

            a_done = B.at("vs")
            for c in range(IC):
                B.dma("sync", "sfm", fmbuf[:, 512 * c:512 * (c + 1)],
                      hq_img.ap()[t0:t0 + 512, 128 * c:128 * (c + 1)],
                      waits=[hq_wr[BPS * sb + BPS - 1],
                             ("ts", B.cnt["ts"])], transpose=True)
            hqt_done = B.at("sfm")
            prev_fm_user[0] = 0

            B.comp("vector", "vs", lambda e: e.memset(maxac[:], 0.0))
            B.comp("vector", "vs", lambda e: e.memset(minac[:], 0.0))
            v_am0 = B.at("vs")

            g_evac = {0: a_done, 1: a_done}
            u_evac = {0: B.at("vs"), 1: B.at("vs")}
            for ob in range(OB):
                wub = wuT[ob % 2]
                ww = all_img_done + [("ts", wu_busy[ob % 2])]
                for c in range(IC):
                    B.dma("sync", f"su{ob % 2}", wub[:, 512 * c:512 * (c + 1)],
                          wu_img.ap()[512 * ob:512 * (ob + 1),
                                      128 * c:128 * (c + 1)],
                          waits=ww if c == 0 else [], transpose=True)
                wu_ld = B.at(f"su{ob % 2}")
                for oc in range(4):
                    g = 4 * ob + oc
                    pg, pu = PB[g % 2], PB[2 + g % 2]
                    for c in range(IC):
                        B.comp("tensor", "ts", lambda e, c=c, g=g, pg=pg: e.matmul(
                            pg[:, :],
                            wqTg[:, INNER * c + 128 * g:INNER * c + 128 * (g + 1)],
                            fmbuf[:, 512 * c:512 * (c + 1)],
                            start=(c == 0), stop=(c == IC - 1)),
                            waits=([hqt_done, wqtg_done, g_evac[g % 2]]
                                   if c == 0 else []),
                            inc=(1 if c == IC - 1 else 0))
                    t_g = B.at("ts")
                    for c in range(IC):
                        B.comp("tensor", "ts", lambda e, c=c, oc=oc, pu=pu, wub=wub:
                               e.matmul(
                            pu[:, :],
                            wub[:, 512 * c + 128 * oc:512 * c + 128 * (oc + 1)],
                            fmbuf[:, 512 * c:512 * (c + 1)],
                            start=(c == 0), stop=(c == IC - 1)),
                            waits=([wu_ld, u_evac[g % 2]] if c == 0 else []),
                            inc=(1 if c == IC - 1 else 0))
                    t_u = B.at("ts")
                    wu_busy[ob % 2] = B.cnt["ts"]
                    prev_fm_user[0] = B.cnt["ts"]
                    B.comp("vector", "vs", lambda e, pg=pg: e.tensor_mul(
                        out=gated[:], in0=pg[:, :], in1=cgb[:]), waits=[t_g],
                        sw=True)
                    g_evac[g % 2] = B.at("vs")
                    B.comp("scalar", "ss", lambda e: e.activation(
                        out=silu_g[:], in_=gated[:], func=AF.Silu, bias=0.0,
                        scale=1.0), waits=[B.at("vs")], sw=False)
                    B.comp("vector", "vs", lambda e, pu=pu, g=g: e.tensor_mul(
                        out=P_sb[:, 512 * g:512 * (g + 1)], in0=pu[:, :],
                        in1=silu_g[:]), waits=[t_u, B.at("ss")], sw=True)
                    u_evac[g % 2] = B.at("vs")
                    B.comp("vector", "vs", lambda e, g=g: e.tensor_tensor(
                        out=maxac[:], in0=maxac[:],
                        in1=P_sb[:, 512 * g:512 * (g + 1)], op=ALU.max),
                        waits=[v_am0])
                    B.comp("vector", "vs", lambda e, g=g: e.tensor_tensor(
                        out=minac[:], in0=minac[:],
                        in1=P_sb[:, 512 * g:512 * (g + 1)], op=ALU.min))
            if debug:
                B.comp("vector", "vs", lambda e, sb=sb: e.tensor_copy(
                    out=gated[:], in_=P_sb[:, 0:512]))
                B.dma("sync", "dbg", P_dbg.ap()[sb, :, 0:512], gated[:],
                      waits=[B.at("vs")])
            B.comp("vector", "vs", lambda e: e.tensor_scalar(
                out=amax_acc[:], in0=minac[:], scalar1=-1.0, scalar2=None,
                op0=ALU.mult))
            B.comp("vector", "vs", lambda e: e.tensor_tensor(
                out=amax_acc[:], in0=amax_acc[:], in1=maxac[:], op=ALU.max))
            g_amax = B.at("vs")

            for tb in range(BPS):
                B.comp("tensor", "ts", lambda e, tb=tb: e.transpose(
                    PB[6 + tb % 2][:, 0:128], amax_acc[:, 128 * tb:128 * (tb + 1)],
                    ident_f[:]), waits=[g_amax, ("vs", B.cnt["vs"])])
                B.comp("vector", "vs", lambda e, tb=tb: e.reduce_max(
                    out=amp_sb[:, tb:tb + 1], in_=PB[6 + tb % 2][:, 0:128], axis=X),
                    waits=[B.at("ts")])
                B.comp("vector", "vs", lambda e, tb=tb: e.tensor_scalar(
                    out=col["t0"][:], in0=amp_sb[:, tb:tb + 1],
                    scalar1=cu_sb[:, tb:tb + 1], scalar2=1e-5,
                    op0=ALU.mult, op1=ALU.max))
                B.comp("vector", "vs", lambda e: e.reciprocal(
                    out=col["t1"][:], in_=col["t0"][:]))
                B.comp("vector", "vs", lambda e, tb=tb: e.tensor_scalar(
                    out=s2_sb[:, tb:tb + 1], in0=col["t1"][:], scalar1=127.0,
                    scalar2=None, op0=ALU.mult))
                B.comp("vector", "vs", lambda e, tb=tb: e.reciprocal(
                    out=s2i_sb[:, tb:tb + 1], in_=s2_sb[:, tb:tb + 1]))
                B.comp("vector", "vs", lambda e, tb=tb: e.tensor_mul(
                    out=qm_sb[:, tb:tb + 1], in0=cu_sb[:, tb:tb + 1],
                    in1=s2_sb[:, tb:tb + 1]))
                B.comp("vector", "vs", lambda e, tb=tb, w=sirep[2]: e.tensor_scalar(
                    out=cd_sb[:, tb:tb + 1], in0=s2i_sb[:, tb:tb + 1],
                    scalar1=w[:], scalar2=None, op0=ALU.mult))
                B.comp("vector", "vs", lambda e, tb=tb: e.tensor_scalar(
                    out=bb_scr[:], in0=ones_f[:], scalar1=qm_sb[:, tb:tb + 1],
                    scalar2=None, op0=ALU.mult))
                B.comp("tensor", "ts", lambda e, tb=tb: e.transpose(
                    PB[6 + tb % 2][:, 0:128], bb_scr[:], ident_f[:]),
                    waits=[("vs", B.cnt["vs"])])
                B.comp("vector", "vs", lambda e, tb=tb: e.tensor_copy(
                    out=qmb[:, 128 * tb:128 * (tb + 1)],
                    in_=PB[6 + tb % 2][:, 0:128]), waits=[B.at("ts")])
            v_qmb = B.at("vs")

            v_psum_free = ("vs", B.cnt["vs"])
            uq = [gated, u1_s]
            iq_ct = 0
            for g in range(OC):
                wb = wdT[g % 2]
                B.dma("scalar", f"sd{g % 2}", wb[:],
                      wd_img.ap()[0:D, 128 * g:128 * (g + 1)],
                      waits=all_img_done + [("ts", wd_busy[g % 2])],
                      transpose=True)
                wd_ld = B.at(f"sd{g % 2}")
                B.comp("gpsimd", "gs", lambda e, g=g, u=uq[g % 2]: e.tensor_mul(
                    out=u[:], in0=P_sb[:, 512 * g:512 * (g + 1)], in1=qmb[:]),
                    waits=[v_qmb], sw=False)
                B.comp("vector", "vs", lambda e, g=g, u=uq[g % 2]: e.tensor_scalar(
                    out=P_sb[:, 512 * g:512 * (g + 1)], in0=u[:], scalar1=MAGIC,
                    scalar2=MAGIC, op0=ALU.add, op1=ALU.subtract),
                    waits=[B.at("gs")], sw=True)
                q3_at = B.at("vs")
                if debug and g == 0:
                    B.comp("vector", "vs", lambda e, sb=sb: e.tensor_copy(
                        out=bb_scr[:], in_=P_sb[:, 0:128]), waits=[q3_at])
                    B.dma("sync", "dbg", iq_dbg.ap()[sb, :, 0:128], bb_scr[:],
                          waits=[B.at("vs")])
                for tb in range(BPS):
                    for jh in range(2):
                        B.comp("tensor", "ts", lambda e, g=g, tb=tb, jh=jh, wb=wb:
                               e.matmul(
                            PB[2 * tb + jh][:, :],
                            P_sb[:, 512 * g + 128 * tb:512 * g + 128 * (tb + 1)],
                            wb[:, 512 * jh:512 * (jh + 1)],
                            start=(g == 0), stop=(g == OC - 1)),
                            waits=([wd_ld, q3_at, v_psum_free]
                                   if (tb == 0 and jh == 0) else []),
                            inc=(1 if (tb == BPS - 1 and jh == 1) else 0))
                wd_busy[g % 2] = B.cnt["ts"]
            t_down = B.at("ts")

            for tb in range(BPS):
                for jh in range(2):
                    B.comp("vector", "vs", lambda e, tb=tb, jh=jh:
                           e.scalar_tensor_tensor(
                        out=r_t[:, 512 * jh:512 * (jh + 1)],
                        in0=PB[2 * tb + jh][:, :], scalar=cd_sb[:, tb:tb + 1],
                        in1=h_sb[:, D * tb + 512 * jh:D * tb + 512 * (jh + 1)],
                        op0=ALU.mult, op1=ALU.add),
                        waits=([t_down] if (tb == 0 and jh == 0) else []))
                B.comp("vector", "vs", lambda e: e.reduce_sum(
                    out=col["sum"][:], in_=r_t[:], axis=X))
                B.comp("vector", "vs", lambda e: e.tensor_scalar(
                    out=col["mean"][:], in0=col["sum"][:], scalar1=float(2.0 ** -10),
                    scalar2=None, op0=ALU.mult))
                B.comp("scalar", "ss", lambda e: e.activation(
                    out=u_scr[:], in_=r_t[:], func=AF.Square, bias=0.0, scale=1.0,
                    accum_out=col["sq"][:]), waits=[B.at("vs")])
                B.comp("vector", "vs", lambda e: e.tensor_scalar(
                    out=col["msq"][:], in0=col["sq"][:], scalar1=float(2.0 ** -10),
                    scalar2=None, op0=ALU.mult), waits=[B.at("ss")])
                B.comp("vector", "vs", lambda e: e.tensor_mul(
                    out=col["m2c"][:], in0=col["mean"][:], in1=col["mean"][:]))
                B.comp("vector", "vs", lambda e: e.tensor_sub(
                    out=col["var"][:], in0=col["msq"][:], in1=col["m2c"][:]))
                B.comp("scalar", "ss", lambda e: e.activation(
                    out=col["stdv"][:], in_=col["var"][:], func=AF.Sqrt,
                    bias=eps_col[:], scale=1.0), waits=[B.at("vs")])
                B.comp("vector", "vs", lambda e: e.reciprocal(
                    out=col["rstd"][:], in_=col["stdv"][:]), waits=[B.at("ss")])
                ysl = (BPS * sb + tb) % 2
                yt = ytile[ysl]
                B.comp("vector", "vs", lambda e, yt=yt: e.tensor_scalar(
                    out=yt[:], in0=r_t[:], scalar1=col["mean"][:],
                    scalar2=col["rstd"][:], op0=ALU.subtract, op1=ALU.mult),
                    waits=([ystore[ysl]] if ysl in ystore else []))
                B.dma("sync", f"sy{ysl}", yo[t0 + 128 * tb:t0 + 128 * (tb + 1), :],
                      yt[:], waits=[B.at("vs")])
                ystore[ysl] = B.at(f"sy{ysl}")

        B.ev("sync", None, waits=[B.at("sy0"), B.at("sy1")])

        with nc.Block() as block:
            def run_engine(ename, engine):
                for eng, fn, waits, incs in B.events:
                    if eng != ename:
                        continue
                    for sname, val in waits:
                        if val > 0:
                            engine.wait_ge(sems[sname], val)
                    if fn is None:
                        continue
                    inst = fn(engine)
                    for sname, n in incs:
                        inst = inst.then_inc(sems[sname], n)

            block.sync(lambda e: run_engine("sync", e))
            block.vector(lambda e: run_engine("vector", e))
            block.scalar(lambda e: run_engine("scalar", e))
            block.tensor(lambda e: run_engine("tensor", e))
            block.gpsimd(lambda e: run_engine("gpsimd", e))

    return nc


_NC_CACHE = [None]


def kernel(L0_attn_x, L0_attn_W, L0_attn_b, L0_ffn_w_gate, L0_ffn_w_up,
           L0_ffn_w_down):
    x = np.ascontiguousarray(np.asarray(L0_attn_x, dtype=np.float32))
    Wm = np.ascontiguousarray(np.asarray(L0_attn_W, dtype=np.float32))
    bv = np.asarray(L0_attn_b, dtype=np.float32)
    wg = np.ascontiguousarray(np.asarray(L0_ffn_w_gate, dtype=np.float32))
    wu = np.ascontiguousarray(np.asarray(L0_ffn_w_up, dtype=np.float32))
    wd = np.ascontiguousarray(np.asarray(L0_ffn_w_down, dtype=np.float32))
    Bn, Ln, Dn = x.shape
    assert (Bn, Ln, Dn) == (2, 8192, 1024)

    bb = np.ascontiguousarray(np.broadcast_to(bv[None, :], (128, K)))
    in_maps = []
    for core in range(8):
        b, q = divmod(core, 4)
        t0 = q * T
        shard = np.zeros((TP, D), dtype=np.float32)
        end = min(Ln, t0 + TP)
        shard[0:end - t0] = x[b, t0:end]
        in_maps.append({"xs": shard, "Wm": Wm, "bb": bb,
                        "wg": wg, "wu": wu, "wd": wd})

    if _NC_CACHE[0] is None:
        _NC_CACHE[0] = build_core_program()
    nc = _NC_CACHE[0]
    res = run_bass_kernel_spmd(nc, in_maps, core_ids=list(range(8)))
    out = np.empty((2, 8192, 1024), dtype=np.float32)
    for core in range(8):
        b, q = divmod(core, 4)
        out[b, q * T:(q + 1) * T] = res.results[core]["yo"]
    return out


if __name__ == "__main__":
    pass
